# version 07
# baseline (speedup 1.0000x reference)
import numpy as np
import concourse.bass as bass
import concourse.mybir as mybir

F32 = mybir.dt.float32
BF16 = mybir.dt.bfloat16
U8 = mybir.dt.uint8
F16 = mybir.dt.float16
AF = mybir.ActivationFunctionType
ALU = mybir.AluOpType
AX = mybir.AxisListType

D = 4096
KD = 32
OWN = 1024
HALO = 16
TW = OWN + 2 * HALO
EPS = 1e-5
ENGINES = ("sync", "tensor", "vector", "scalar", "gpsimd")
ARENA_BYTES = 155 * 1024


def chunks(width, maxw=512):
    out = []
    o = 0
    while o < width:
        w = min(maxw, width - o)
        out.append((o, w))
        o += w
    return out


def tile_lhsT(w_km, ogm):
    K, M = w_km.shape
    KT = K // 128
    OG = M // ogm
    return np.ascontiguousarray(
        w_km.reshape(KT, 128, OG, ogm).transpose(2, 1, 0, 3).reshape(OG, 128, KT * ogm))


class Sem:
    def __init__(self, h):
        self.h = h
        self.n = 0

    def bump(self, k):
        self.n += k
        return self.n


class B:
    def __init__(self, nc, width):
        self.nc = nc
        self.W = width
        self.ops = {e: [] for e in ENGINES}
        self.sems = {}
        for name in ("sp", "sv", "ss", "sc") + tuple(f"q{i}" for i in range(16)):
            self.sems[name] = Sem(nc.alloc_semaphore(name))
        self.fixed()

    ESEM = {"vector": "sv", "scalar": "ss", "gpsimd": "sc"}

    def op(self, eng, fn):
        es = self.ESEM.get(eng)
        if es is not None:
            return self.op_inc(eng, fn, es, 1)
        self.ops[eng].append(fn)
        return None

    CHAIN_WAIT = {"vector": True, "scalar": True, "gpsimd": True}

    def chain(self, eng, fn):
        if self.CHAIN_WAIT.get(eng, True):
            es = self.ESEM[eng]
            self.w(eng, es, self.sems[es].n)
        return self.op(eng, fn)

    def w(self, eng, sem, val):
        if val and val > 0:
            self.ops[eng].append(lambda h, s=self.sems[sem].h, v=val: h.wait_ge(s, v))

    def op_inc(self, eng, fn, sem, k=1):
        s = self.sems[sem]

        def g(h, fn=fn, hh=s.h, k=k):
            fn(h).then_inc(hh, k)
        self.ops[eng].append(g)
        return s.bump(k)

    def snap(self):
        return {k: s.n for k, s in self.sems.items()}

    def barrier(self, engs, snap=None):
        snap = snap or self.snap()
        for e in engs:
            for k, v in snap.items():
                self.w(e, k, v)
        return snap

    def build(self, block):
        for eng in ENGINES:
            fns = self.ops[eng]
            section = getattr(block, eng)

            def body(h, fns=fns):
                for f in fns:
                    f(h)
            section(body)

    def fixed(self):
        nc = self.nc
        W = self.W
        self.PS = nc.alloc_psum_tensor("PS", [128, 8, 512], F32).ap()
        self.MBs = nc.alloc_sbuf_tensor("MBs", [128, W], F32).ap()
        self.RBs = nc.alloc_sbuf_tensor("RBs", [128, W], F32).ap()
        self.ring_x = nc.alloc_sbuf_tensor("ring_x", [128, 2, W], F32).ap()
        self.ring_c = nc.alloc_sbuf_tensor("ring_c", [128, 2, W], F32).ap()
        self.ring_a = nc.alloc_sbuf_tensor("ring_a", [128, 2, W], BF16).ap()
        self.ring_b = nc.alloc_sbuf_tensor("ring_b", [128, 2, W], BF16).ap()
        self.ones_col = nc.alloc_sbuf_tensor("ones_col", [128, 1], BF16).ap()
        self.ones_row32 = nc.alloc_sbuf_tensor("ones_row32", [1, 128], F32).ap()
        self.ones_row16 = nc.alloc_sbuf_tensor("ones_row16", [1, 512], BF16).ap()
        self.ident16 = nc.alloc_sbuf_tensor("ident16", [128, 128], BF16).ap()
        self.ident32 = nc.alloc_sbuf_tensor("ident32", [128, 128], F32).ap()
        self.causal = nc.alloc_sbuf_tensor("causal", [128, 128], F32).ap()
        self.srow = nc.alloc_sbuf_tensor("srow", [1, 4, TW], F32).ap()
        self.arena_t = nc.alloc_sbuf_tensor("AR", [128, ARENA_BYTES], U8)
        self.ar_off = 0
        self.ar_marks = []
        self.ar_hi = ARENA_BYTES

    def carve_high(self, dtype, shape):
        esz = mybir.dt.size(dtype)
        n = int(np.prod(shape)) * esz
        off = (self.ar_hi - n) // 512 * 512
        assert off >= self.ar_off, f"arena high overflow {off} < {self.ar_off}"
        self.ar_hi = off
        ap = self.arena_t.ap()[:, off:off + n].bitcast(dtype)
        if len(shape) > 1:
            spec = " ".join(f"a{i}" for i in range(len(shape)))
            kw = {f"a{i}": int(shape[i]) for i in range(len(shape))}
            ap = ap.rearrange(f"p ({spec}) -> p {spec}", **kw)
        return ap

    def release_high(self):
        self.ar_hi = ARENA_BYTES

    def carve(self, dtype, shape):
        esz = mybir.dt.size(dtype)
        n = int(np.prod(shape)) * esz
        off = (self.ar_off + 511) // 512 * 512
        assert off + n <= self.ar_hi, f"arena overflow {off}+{n} hi={self.ar_hi}"
        self.ar_off = off + n
        ap = self.arena_t.ap()[:, off:off + n].bitcast(dtype)
        if len(shape) > 1:
            spec = " ".join(f"a{i}" for i in range(len(shape)))
            kw = {f"a{i}": int(shape[i]) for i in range(len(shape))}
            ap = ap.rearrange(f"p ({spec}) -> p {spec}", **kw)
        return ap

    def mark(self):
        self.ar_marks.append(self.ar_off)

    def release(self):
        self.ar_off = self.ar_marks.pop()

    def emit_init(self):
        v = None
        for a, val in ((self.ones_col, 1.0), (self.ones_row32, 1.0), (self.ones_row16, 1.0),
                       (self.ident16, 1.0), (self.ident32, 1.0), (self.causal, 0.0)):
            v = self.op_inc("vector", lambda h, a=a, vv=val: h.memset(a, vv), "sv", 1)
        self.w("gpsimd", "sv", v)
        g = "gpsimd"
        self.op(g, lambda h, a=self.ident16: h.affine_select(
            a, a, pattern=[[1, 128]], base=0, channel_multiplier=-1,
            compare_op=ALU.is_equal, fill=0.0))
        self.op(g, lambda h, a=self.ident32: h.affine_select(
            a, a, pattern=[[1, 128]], base=0, channel_multiplier=-1,
            compare_op=ALU.is_equal, fill=0.0))
        self.op_inc(g, lambda h, a=self.causal: h.affine_select(
            a, a, pattern=[[-1, 128]], base=0, channel_multiplier=1,
            compare_op=ALU.is_ge, fill=-1e9), "sc", 1)
        self.barrier(ENGINES)

    def emit_fm_stats(self, tile_fn, n_kt, width, src_f32):
        self.barrier(("sync", "scalar", "tensor"))
        chs = chunks(width)
        G = 4
        self.mark()
        rx = self.carve(F32, (G, width)) if src_f32 else None
        ra = self.carve(BF16, (G, width))
        rb = self.carve(BF16, (G, width))
        done = {}
        sdone = {}
        for g0 in range(0, n_kt, G):
            grp = list(range(g0, min(g0 + G, n_kt)))
            for kt in grp:
                slot = kt % G
                if kt >= G:
                    self.w("sync", "ss", sdone[kt - G])
                    self.w("sync", "sp", done[kt - G])
                dst = rx[:, slot, :] if src_f32 else ra[:, slot, :]
                self.op_inc("sync", lambda h, d=dst, kt=kt: h.dma_start(
                    out=d, in_=tile_fn(kt)), f"q{slot}", 16)
            for kt in grp:
                slot = kt % G
                if kt >= G:
                    self.w("scalar", "sp", done[kt - G])
                self.w("scalar", f"q{slot}", self.sems[f"q{slot}"].n)
                if src_f32:
                    self.op("scalar", lambda h, o=ra[:, slot, :], i=rx[:, slot, :]: h.copy(o, i))
            for kt in grp:
                slot = kt % G
                xa = ra[:, slot, :]
                if src_f32:
                    self.w("scalar", "ss", self.sems["ss"].n - len(grp) + (kt - g0) + 1)
                sdone[kt] = self.op("scalar", lambda h, o=rb[:, slot, :], i=xa: h.square(o, i))
            for kt in grp:
                slot = kt % G
                self.w("tensor", "ss", sdone[kt])
                xa = ra[:, slot, :]
                xsq = rb[:, slot, :]
                for ci, (o, wd) in enumerate(chs):
                    self.op("tensor", lambda h, p=self.PS[0:1, ci, 0:wd], r=xa[:, o:o + wd],
                            st=(kt == 0), sp_=(kt == n_kt - 1):
                            h.matmul(p, self.ones_col, r, start=st, stop=sp_))
                for ci, (o, wd) in enumerate(chs):
                    fn = lambda h, p=self.PS[0:1, 3 + ci, 0:wd], r=xsq[:, o:o + wd], \
                        st=(kt == 0), sp_=(kt == n_kt - 1): \
                        h.matmul(p, self.ones_col, r, start=st, stop=sp_)
                    if ci == len(chs) - 1:
                        done[kt] = self.op_inc("tensor", fn, "sp", 1)
                    else:
                        self.op("tensor", fn)
        self.release()

    def finalize_stats(self, width):
        chs = chunks(width)
        self.barrier(("vector",))
        m_row = self.srow[:, 0, 0:width]
        v_row = self.srow[:, 1, 0:width]
        t_row = self.srow[:, 3, 0:width]
        inv = 1.0 / 4096.0
        for ci, (o, wd) in enumerate(chs):
            self.op("vector", lambda h, d=m_row[:, o:o + wd], s=self.PS[0:1, ci, 0:wd]:
                    h.tensor_scalar(d, s, inv, None, op0=ALU.mult))
            self.op("vector", lambda h, d=v_row[:, o:o + wd], s=self.PS[0:1, 3 + ci, 0:wd]:
                    h.tensor_scalar(d, s, inv, None, op0=ALU.mult))
        self.chain("vector", lambda h, d=t_row, s=m_row: h.tensor_tensor(d, s, s, op=ALU.mult))
        self.chain("vector", lambda h, d=v_row, s0=v_row, s1=t_row: h.tensor_tensor(d, s0, s1, op=ALU.subtract))
        vv = self.chain("vector", lambda h, d=v_row, s=v_row: h.tensor_scalar(
            d, s, EPS, None, op0=ALU.add))
        self.w("scalar", "sv", vv)
        sq = self.op("scalar", lambda h, d=t_row, s=v_row: h.sqrt(d, s))
        self.w("vector", "ss", sq)
        self.op("vector", lambda h, d=v_row, s=t_row: h.reciprocal(d, s))
        nm = self.chain("vector", lambda h, d=m_row, s=m_row: h.tensor_scalar(
            d, s, -1.0, None, op0=ALU.mult))
        self.w("tensor", "sv", nm)
        for ci, (o, wd) in enumerate(chs):
            self.op("tensor", lambda h, p=self.PS[:, ci, 0:wd], r=m_row[:, o:o + wd]:
                    h.matmul(p, self.ones_row32, r, start=True, stop=True))
        pv = None
        for ci, (o, wd) in enumerate(chs):
            fn = lambda h, p=self.PS[:, 3 + ci, 0:wd], r=v_row[:, o:o + wd]: \
                h.matmul(p, self.ones_row32, r, start=True, stop=True)
            if ci == len(chs) - 1:
                pv = self.op_inc("tensor", fn, "sp", 1)
            else:
                self.op("tensor", fn)
        self.w("vector", "sp", pv)
        for ci, (o, wd) in enumerate(chs):
            self.op("vector", lambda h, d=self.MBs[:, o:o + wd], s=self.PS[:, ci, 0:wd]:
                    h.tensor_copy(d, s))
        for ci, (o, wd) in enumerate(chs):
            self.op("vector", lambda h, d=self.RBs[:, o:o + wd], s=self.PS[:, 3 + ci, 0:wd]:
                        h.tensor_copy(d, s))

    def emit_ln_apply(self, tile_fn, n_kt, width, dst_fn, also_dram_fn=None):
        self.barrier(("sync", "vector"))
        G = 4
        self.mark()
        rx = self.carve(F32, (G, width))
        rc = self.carve(F32, (G, width))
        vdone = {}
        adv = {}
        for g0 in range(0, n_kt, G):
            grp = list(range(g0, min(g0 + G, n_kt)))
            for kt in grp:
                slot = kt % G
                if kt >= G:
                    self.w("sync", "sv", vdone[kt - G])
                self.op_inc("sync", lambda h, d=rx[:, slot, :], kt=kt: h.dma_start(
                    out=d, in_=tile_fn(kt)), f"q{slot}", 16)
            for kt in grp:
                slot = kt % G
                self.w("vector", f"q{slot}", self.sems[f"q{slot}"].n)
                self.op("vector", lambda h, d=rc[:, slot, :], s=rx[:, slot, :]: h.tensor_tensor(
                    d, s, self.MBs[:, 0:width], op=ALU.add))
            for kt in grp:
                slot = kt % G
                if also_dram_fn is not None and kt >= 4:
                    self.w("vector", f"q{8 + kt % 4}", adv[kt - 4])
                self.w("vector", "sv", self.sems["sv"].n - len(grp) + (kt - g0) + 1)
                vdone[kt] = self.op("vector", lambda h, d=dst_fn(kt), s=rc[:, slot, :]: h.tensor_tensor(
                    d, s, self.RBs[:, 0:width], op=ALU.mult))
            if also_dram_fn is not None:
                for kt in grp:
                    self.w("scalar", "sv", vdone[kt])
                    adv[kt] = self.op_inc("scalar", lambda h, d=also_dram_fn(kt), s=dst_fn(kt): h.dma_start(
                        out=d, in_=s), f"q{8 + kt % 4}", 16)
        self.release()

    def emit_resid(self, x_tile_fn, out_tile_fn, n_kt, width, dst_tile_fn,
                   next_stats=False):
        self.barrier(("sync", "vector", "scalar", "tensor"))
        chs = chunks(width)
        G = 4
        self.mark()
        rx = self.carve(F32, (G, width))
        ra = self.carve(BF16, (G, width))
        rc = self.carve(F32, (G, width))
        rbq = self.carve(BF16, (G, width)) if next_stats else None
        rsq = self.carve(BF16, (G, width)) if next_stats else None
        vdone = {}
        stv = {}
        sdone = {}
        tdone = {}
        for g0 in range(0, n_kt, G):
            grp = list(range(g0, min(g0 + G, n_kt)))
            for kt in grp:
                slot = kt % G
                if kt >= G:
                    self.w("sync", "sv", vdone[kt - G])
                    self.w("sync", f"q{8 + kt % 4}", stv[kt - G])
                    if next_stats:
                        self.w("sync", "ss", sdone[kt - G])
                self.op_inc("sync", lambda h, d=rx[:, slot, :], kt=kt: h.dma_start(
                    out=d, in_=x_tile_fn(kt)), f"q{slot}", 16)
            for kt in grp:
                slot = kt % G
                if kt >= G and next_stats:
                    self.w("sync", "sp", tdone[kt - G])
                self.op_inc("sync", lambda h, d=ra[:, slot, :], kt=kt: h.dma_start(
                    out=d, in_=out_tile_fn(kt)), f"q{4 + slot}", 16)
            for kt in grp:
                slot = kt % G
                self.w("vector", f"q{slot}", self.sems[f"q{slot}"].n)
                self.w("vector", f"q{4 + slot}", self.sems[f"q{4 + slot}"].n)
                self.op("vector", lambda h, d=rc[:, slot, :], s=ra[:, slot, :]: h.tensor_tensor(
                    d, s, self.MBs[:, 0:width], op=ALU.add))
            for kt in grp:
                slot = kt % G
                self.w("vector", "sv", self.sems["sv"].n - len(grp) + (kt - g0) + 1)
                self.op("vector", lambda h, d=rc[:, slot, :], s=rc[:, slot, :]: h.tensor_tensor(
                    d, s, self.RBs[:, 0:width], op=ALU.mult))
            for kt in grp:
                slot = kt % G
                self.w("vector", "sv", self.sems["sv"].n - len(grp) + (kt - g0) + 1)
                vdone[kt] = self.op("vector", lambda h, d=rx[:, slot, :], s=rc[:, slot, :],
                                    x2=rx[:, slot, :]: h.tensor_tensor(d, s, x2, op=ALU.add))
            for kt in grp:
                slot = kt % G
                self.w("sync", "sv", vdone[kt])
                stv[kt] = self.op_inc("sync", lambda h, d=dst_tile_fn(kt), s=rx[:, slot, :]: h.dma_start(
                    out=d, in_=s), f"q{8 + slot}", 16)
            if next_stats:
                for kt in grp:
                    slot = kt % G
                    self.w("scalar", "sv", vdone[kt])
                    if kt >= G:
                        self.w("scalar", "sp", tdone[kt - G])
                    self.op("scalar", lambda h, o=rbq[:, slot, :], i=rx[:, slot, :]: h.copy(o, i))
                for kt in grp:
                    slot = kt % G
                    self.w("scalar", "ss", self.sems["ss"].n - len(grp) + (kt - g0) + 1)
                    sdone[kt] = self.op("scalar", lambda h, o=rsq[:, slot, :], i=rbq[:, slot, :]: h.square(o, i))
                for kt in grp:
                    slot = kt % G
                    self.w("tensor", "ss", sdone[kt])
                    for ci, (o, wd) in enumerate(chs):
                        self.op("tensor", lambda h, p=self.PS[0:1, ci, 0:wd], r=rbq[:, slot, o:o + wd],
                                st=(kt == 0), sp_=(kt == n_kt - 1):
                                h.matmul(p, self.ones_col, r, start=st, stop=sp_))
                    for ci, (o, wd) in enumerate(chs):
                        fn = lambda h, p=self.PS[0:1, 3 + ci, 0:wd], r=rsq[:, slot, o:o + wd], \
                            st=(kt == 0), sp_=(kt == n_kt - 1): \
                            h.matmul(p, self.ones_col, r, start=st, stop=sp_)
                        if ci == len(chs) - 1:
                            tdone[kt] = self.op_inc("tensor", fn, "sp", 1)
                        else:
                            self.op("tensor", fn)
        self.release()

    def stream_mm(self, w_dram, OG, KT, rhs_fn, evac_fn, width):
        chs = chunks(width)
        self.mark()
        wbuf = self.carve(BF16, (2, KT, 128))
        self.barrier(("gpsimd", "tensor"))
        evac_gate = {}
        for og in range(OG):
            slot = og % 2
            bank0 = (og % 2) * 3
            if og >= 2:
                g = evac_gate[og - 2]
                self.w("gpsimd", g["sem"], g["val"])
            gv = self.op_inc("gpsimd", lambda h, d=wbuf[:, slot], og=og: h.dma_start(
                out=d, in_=w_dram[og]), f"q{12 + slot}", 16)
            self.w("tensor", f"q{12 + slot}", gv)
            if og >= 2:
                g = evac_gate[og - 2]
                self.w("tensor", g["sem"], g["val"])
            pv = None
            for ci, (o, wd) in enumerate(chs):
                for k in range(KT):
                    fn = lambda h, p=self.PS[:, bank0 + ci, 0:wd], l=wbuf[:, slot, k, :], \
                        r=rhs_fn(k, o, wd), st=(k == 0), sp_=(k == KT - 1): \
                        h.matmul(p, l, r, start=st, stop=sp_)
                    if ci == len(chs) - 1 and k == KT - 1:
                        pv = self.op_inc("tensor", fn, "sp", 1)
                    else:
                        self.op("tensor", fn)
            evac_gate[og] = evac_fn(og, bank0, pv)
        self.release()

    def emit_block1(self, io, stage="full", pre_stats_done=False):
        W = self.W
        chs = chunks(W)
        xfm = io["xin"]
        xout = io["xout"]
        out_d = io["b1_out"]
        h_d = io["b1_h"]
        fl = lambda a: a.rearrange("p a b -> p (a b)")

        NH = 22
        self.mark()
        lp = self.carve(F32, (8,))
        gv = self.op_inc("gpsimd", lambda h, d=lp: h.dma_start(out=d, in_=io["b1_lp"]), "q12", 16)

        def xtile(kt):
            return xfm[kt * 128:(kt + 1) * 128, :]

        self.mark()
        XLN = self.carve(BF16, (KD, W))
        self.mark()
        U = self.carve(BF16, (8, W))
        delta = self.carve(BF16, (8, W))
        if not pre_stats_done:
            self.emit_fm_stats(xtile, KD, W, True)
        self.finalize_stats(W)
        self.emit_ln_apply(xtile, KD, W, lambda kt: XLN[:, kt, :])
        xln_rhs = lambda k, o, wd: XLN[:, k, o:o + wd]
        if stage == "ln":
            self.dump_sbuf(XLN, io["dbg"], (KD, W))
            return

        def evac_U(og, bank0, done):
            self.w("scalar", "sp", done)
            v = None
            for ci, (o, wd) in enumerate(chs):
                v = self.op_inc("scalar", lambda h, d=U[:, og, o:o + wd],
                                s=self.PS[:, bank0 + ci, 0:wd]: h.copy(d, s), "ss", 1)
            return {"sem": "ss", "val": v}

        def evac_G(og, bank0, done):
            self.w("vector", "sp", done)
            self.w("vector", "q12", gv)
            v = None
            for ci, (o, wd) in enumerate(chs):
                v = self.op_inc("vector", lambda h, d=delta[:, og, o:o + wd],
                                s=self.PS[:, bank0 + ci, 0:wd], og=og: h.tensor_scalar(
                    d, s, lp[:, og:og + 1], None, op0=ALU.subtract), "sv", 1)
            return {"sem": "sv", "val": v}

        self.stream_mm(io["b1_up"], 8, KD, xln_rhs, evac_U, W)
        self.stream_mm(io["b1_gate"], 8, KD, xln_rhs, evac_G, W)
        if stage == "ug":
            self.dump_sbuf(U, io["dbg"], (8, W))
            self.dump_sbuf(delta, io["dbg2"], (8, W))
            return
        self.barrier(("sync",))
        u1 = self.op_inc("sync", lambda h, d=io["b1_ud"][0], s=U.rearrange("p a b -> p (a b)"):
                         h.dma_start(out=d, in_=s), "q10", 16)
        u2 = self.op_inc("sync", lambda h, d=io["b1_ud"][1], s=delta.rearrange("p a b -> p (a b)"):
                         h.dma_start(out=d, in_=s), "q11", 16)
        self.barrier(ENGINES)
        self.release()
        hA = self.carve_high(BF16, (NH, W))
        self.emit_b1_ffn(io, XLN, h_d, hA)
        self.release()
        if stage == "ffn":
            return

        self.emit_b1_down(io, h_d, out_d, hA)
        self.release_high()
        if stage == "down":
            return
        U = self.carve(BF16, (8, W))
        delta = self.carve(BF16, (8, W))
        self.barrier(("sync",))
        r1 = self.op_inc("sync", lambda h, d=U.rearrange("p a b -> p (a b)"), s=io["b1_ud"][0]:
                         h.dma_start(out=d, in_=s), "q10", 16)
        r2 = self.op_inc("sync", lambda h, d=delta.rearrange("p a b -> p (a b)"), s=io["b1_ud"][1]:
                         h.dma_start(out=d, in_=s), "q11", 16)
        self.w("scalar", "q10", r1)
        self.w("scalar", "q11", r2)

        dmask = self.carve(BF16, (8, W))
        t_ch = self.carve(BF16, (8, W))
        self.barrier(("scalar", "vector"))
        av = self.op("scalar", lambda h, d=t_ch, s=delta: h.activation(
            fl(d), fl(s), AF.Abs))
        self.w("vector", "ss", av)
        self.op("vector", lambda h, d=t_ch: h.tensor_scalar(
            fl(d), fl(d), 2.5, None, op0=ALU.is_le))
        self.chain("vector", lambda h, d=dmask, s0=delta, s1=t_ch: h.tensor_tensor(
            fl(d), fl(s0), fl(s1), op=ALU.mult))

        self.mark()
        UA = self.carve(BF16, (24, W))
        self.barrier(("vector",))
        self.op("vector", lambda h, d=UA[:, 0:8], s=U: h.tensor_copy(fl(d), fl(s)))
        self.op("vector", lambda h, d=UA[:, 8:16], s0=U, s1=delta: h.tensor_tensor(
            fl(d), fl(s0), fl(s1), op=ALU.mult))
        self.chain("vector", lambda h, d=t_ch, s=delta: h.scalar_tensor_tensor(
            fl(d), fl(s), 0.5, fl(delta), op0=ALU.mult, op1=ALU.mult))
        self.chain("vector", lambda h, d=UA[:, 16:24], s0=U, s1=t_ch: h.tensor_tensor(
            fl(d), fl(s0), fl(s1), op=ALU.mult))
        self.emit_apx(io["b1_apxA"], 24, lambda k, o, wd: UA[:, k, o:o + wd], out_d)
        self.release()
        if stage == "apxA":
            return

        self.mark()
        UB = self.carve(BF16, (32, W))
        self.barrier(("vector",))
        for i, (scal, srcm) in enumerate(((1 / 3, delta), (1 / 4, delta), (1 / 5, dmask), (1 / 6, delta))):
            self.chain("vector", lambda h, d=t_ch, sc=scal, m=srcm: h.scalar_tensor_tensor(
                fl(d), fl(d), sc, fl(m), op0=ALU.mult, op1=ALU.mult))
            self.chain("vector", lambda h, d=UB[:, i * 8:(i + 1) * 8], s0=U, s1=t_ch: h.tensor_tensor(
                fl(d), fl(s0), fl(s1), op=ALU.mult))
        self.emit_apx(io["b1_apxB"], 32, lambda k, o, wd: UB[:, k, o:o + wd], out_d)
        self.release()
        self.release()
        if stage == "apxB":
            return

        otile = lambda kt: out_d[kt]
        self.emit_fm_stats(otile, KD, W, False)
        if stage == "post0":
            return
        self.finalize_stats(W)
        if stage == "post1":
            return
        self.emit_resid(xtile, otile, KD, W,
                        lambda kt: xout[kt * 128:(kt + 1) * 128, :], next_stats=True)

    def dump_sbuf(self, ap, dram, shape):
        self.barrier(("sync",))
        if len(shape) == 2:
            for i in range(shape[0]):
                self.op_inc("sync", lambda h, d=dram[i], s=ap[:, i, :]: h.dma_start(
                    out=d, in_=s), f"q{i % 2}", 16)
        self.barrier(("sync",))

    def emit_b1_ffn(self, io, XLN, h_d, hA):
        W = self.W
        chs = chunks(W)
        self.mark()
        gbuf = self.carve(BF16, (2, KD, 128))
        ubuf = self.carve(BF16, (2, KD, 128))
        hs = self.carve(BF16, (2, W))
        ht = self.carve(BF16, (2, W))
        self.barrier(("gpsimd", "tensor", "scalar", "vector", "sync"))
        gW = io["b1_gw"]
        uW = io["b1_uw"]
        sdone = {}
        vdone = {}
        stv = {}
        for og in range(78):
            slot = og % 2
            if og >= 2:
                self.w("gpsimd", "ss", sdone[og - 2])
                self.w("gpsimd", "sv", vdone[og - 2])
            g1 = self.op_inc("gpsimd", lambda h, d=gbuf[:, slot], og=og: h.dma_start(
                out=d, in_=gW[og]), f"q{12 + slot}", 16)
            g2 = self.op_inc("gpsimd", lambda h, d=ubuf[:, slot], og=og: h.dma_start(
                out=d, in_=uW[og]), f"q{14 + slot}", 16)
            self.w("tensor", f"q{12 + slot}", g1)
            if og >= 1:
                self.w("tensor", "ss", sdone[og - 1])
            pgv = None
            for ci, (o, wd) in enumerate(chs):
                for k in range(KD):
                    fn = lambda h, p=self.PS[:, ci, 0:wd], l=gbuf[:, slot, k, :], \
                        r=XLN[:, k, o:o + wd], st=(k == 0), sp_=(k == KD - 1): \
                        h.matmul(p, l, r, start=st, stop=sp_)
                    if ci == len(chs) - 1 and k == KD - 1:
                        pgv = self.op_inc("tensor", fn, "sp", 1)
                    else:
                        self.op("tensor", fn)
            self.w("tensor", f"q{14 + slot}", g2)
            if og >= 1:
                self.w("tensor", "sv", vdone[og - 1])
            puv = None
            for ci, (o, wd) in enumerate(chs):
                for k in range(KD):
                    fn = lambda h, p=self.PS[:, 3 + ci, 0:wd], l=ubuf[:, slot, k, :], \
                        r=XLN[:, k, o:o + wd], st=(k == 0), sp_=(k == KD - 1): \
                        h.matmul(p, l, r, start=st, stop=sp_)
                    if ci == len(chs) - 1 and k == KD - 1:
                        puv = self.op_inc("tensor", fn, "sp", 1)
                    else:
                        self.op("tensor", fn)
            self.w("scalar", "sp", pgv)
            if og >= 2:
                self.w("scalar", "sv", vdone[og - 2])
            sv = None
            for ci, (o, wd) in enumerate(chs):
                sv = self.op_inc("scalar", lambda h, d=hs[:, slot, o:o + wd],
                                 s=self.PS[:, ci, 0:wd]: h.activation(d, s, AF.Silu), "ss", 1)
            sdone[og] = sv
            self.w("vector", "sp", puv)
            self.w("vector", "ss", sv)
            if og >= 2:
                self.w("vector", f"q{4 + slot}", stv[og - 2])
            vv = None
            hdst = hA[:, og, :] if og < 22 else ht[:, slot, :]
            for ci, (o, wd) in enumerate(chs):
                vv = self.op_inc("vector", lambda h, d=hdst[:, o:o + wd],
                                 s0=hs[:, slot, o:o + wd], s1=self.PS[:, 3 + ci, 0:wd]:
                                 h.tensor_tensor(d, s0, s1, op=ALU.mult), "sv", 1)
            vdone[og] = vv
            if og >= 22:
                self.w("sync", "sv", vv)
                stv[og] = self.op_inc("sync", lambda h, d=h_d[og], s=ht[:, slot]: h.dma_start(
                    out=d, in_=s), f"q{4 + slot}", 16)
            else:
                stv[og] = None
        self.release()

    SERIAL_DOWN = False

    def emit_b1_down(self, io, h_d, out_d, hA):
        W = self.W
        chs = chunks(W)
        self.mark()
        dbuf = self.carve(BF16, (2, 78, 256))
        hring = self.carve(BF16, (8, W))
        self.barrier(("gpsimd", "tensor", "scalar", "sync"))
        dW = io["b1_dw"]
        edone = {}
        kdone = {}
        stv = {}
        for og in range(16):
            slot = og % 2
            if self.SERIAL_DOWN:
                self.barrier(ENGINES)
            if og >= 2:
                self.w("gpsimd", "ss", edone[og - 2])
            gv = self.op_inc("gpsimd", lambda h, d=dbuf[:, slot], og=og: h.dma_start(
                out=d, in_=dW[og]), f"q{12 + slot}", 16)
            NH = 22
            NS = 78 - NH
            for k in range(78):
                if k >= NH:
                    ks = k - NH
                    hslot = ks % 8
                    lastcong = (NS - 1) - ((NS - 1 - hslot) % 8)
                    prev = (og, NH + ks - 8) if ks >= 8 else (
                        (og - 1, NH + lastcong) if og >= 1 else None)
                    if prev in kdone:
                        self.w("sync", "sp", kdone[prev])
                    hv = self.op_inc("sync", lambda h, d=hring[:, hslot], k=k: h.dma_start(
                        out=d, in_=h_d[k]), f"q{hslot}", 16)
                    self.w("tensor", f"q{hslot}", hv)
                    rtile = hring[:, hslot]
                else:
                    rtile = hA[:, k]
                if k == 0:
                    self.w("tensor", f"q{12 + slot}", gv)
                    if og >= 1:
                        self.w("tensor", "ss", edone[og - 1])
                for ot in range(2):
                    for ci, (o, wd) in enumerate(chs):
                        fn = lambda h, p=self.PS[:, ot * 3 + ci, 0:wd], \
                            l=dbuf[:, slot, k, ot * 128:(ot + 1) * 128], \
                            r=rtile[:, o:o + wd], st=(k == 0), sp_=(k == 77): \
                            h.matmul(p, l, r, start=st, stop=sp_)
                        if ot == 1 and ci == len(chs) - 1:
                            kdone[(og, k)] = self.op_inc("tensor", fn, "sp", 1)
                        else:
                            self.op("tensor", fn)
            pv = kdone[(og, 77)]
            self.w("scalar", "sp", pv)
            ev = None
            for ot in range(2):
                if og >= 1:
                    self.w("scalar", f"q{8 + ot}", stv.get((og - 1, ot)))
                psflat = self.PS[:, ot * 3, 0:512].rearrange("p a -> p a") if False else None
                src_ap = self.PS[:, ot * 3:(ot + 1) * 3, :].rearrange("p a b -> p (a b)")[:, 0:W]
                ev = self.op_inc("scalar", lambda h, d=self.ring_a[:, ot, 0:W],
                                 s=src_ap: h.copy(d, s), "ss", 1)
                self.w("scalar", "ss", ev)
                stv[(og, ot)] = self.op_inc("scalar", lambda h, d=out_d[og * 2 + ot],
                                            s=self.ring_a[:, ot]: h.dma_start(out=d, in_=s), f"q{8 + ot}", 16)
            edone[og] = ev
        self.release()

    def emit_apx(self, w_dram, KT, rhs_fn, out_d):
        W = self.W
        chs = chunks(W)
        self.mark()
        wbuf = self.carve(BF16, (2, KT, 128))
        self.barrier(("gpsimd", "tensor", "vector", "sync", "scalar"))
        vdone = {}
        stv = {}
        ldv = {}
        for og in range(32):
            slot = og % 2
            bank0 = (og % 2) * 3
            if og >= 2:
                self.w("gpsimd", "sv", vdone[og - 2])
            gv = self.op_inc("gpsimd", lambda h, d=wbuf[:, slot], og=og: h.dma_start(
                out=d, in_=w_dram[og]), f"q{12 + slot}", 16)
            if og >= 2:
                self.w("sync", "sv", vdone[og - 2])
            ldv[og] = self.op_inc("sync", lambda h, d=self.ring_b[:, slot], og=og: h.dma_start(
                out=d, in_=out_d[og]), f"q{2 + slot}", 16)
            self.w("tensor", f"q{12 + slot}", gv)
            if og >= 2:
                self.w("tensor", "sv", vdone[og - 2])
            pv = None
            for ci, (o, wd) in enumerate(chs):
                for k in range(KT):
                    fn = lambda h, p=self.PS[:, bank0 + ci, 0:wd], l=wbuf[:, slot, k, :], \
                        r=rhs_fn(k, o, wd), st=(k == 0), sp_=(k == KT - 1): \
                        h.matmul(p, l, r, start=st, stop=sp_)
                    if ci == len(chs) - 1 and k == KT - 1:
                        pv = self.op_inc("tensor", fn, "sp", 1)
                    else:
                        self.op("tensor", fn)
            self.w("vector", "sp", pv)
            self.w("vector", f"q{2 + slot}", ldv[og])
            if og >= 2:
                self.w("vector", f"q{8 + slot}", stv[og - 2])
            vv = None
            for ci, (o, wd) in enumerate(chs):
                vv = self.op_inc("vector", lambda h, d=self.ring_a[:, slot, o:o + wd],
                                 s0=self.ring_b[:, slot, o:o + wd],
                                 s1=self.PS[:, bank0 + ci, 0:wd]:
                                 h.tensor_tensor(d, s1, s0, op=ALU.add), "sv", 1)
            vdone[og] = vv
            self.w("scalar", "sv", vv)
            stv[og] = self.op_inc("scalar", lambda h, d=out_d[og], s=self.ring_a[:, slot]:
                                  h.dma_start(out=d, in_=s), f"q{8 + slot}", 16)
        self.release()


def prep_b1(inp):
    up = np.ascontiguousarray(inp["L0_attn_up_w"].T)
    gate = np.ascontiguousarray(inp["L0_attn_gate_w"].T)
    lao = np.ascontiguousarray(inp["L0_attn_local_approx_output"].T)
    fuse = inp["L0_attn_fuse_weight"]
    fT = [np.ascontiguousarray(fuse[:, :, i].T) for i in range(6)]
    apxA = np.concatenate([lao, fT[0], fT[1]], axis=0)
    apxB = np.concatenate([fT[2], fT[3], fT[4], fT[5]], axis=0)
    return {
        "b1_up": tile_lhsT(up, 128),
        "b1_gate": tile_lhsT(gate, 128),
        "b1_lp": np.ascontiguousarray(inp["L0_attn_local_point"].reshape(8, 128).T),
        "b1_gw": tile_lhsT(np.ascontiguousarray(inp["L0_attn_ffn_gate_w"].T), 128),
        "b1_uw": tile_lhsT(np.ascontiguousarray(inp["L0_attn_ffn_up_w"].T), 128),
        "b1_dw": tile_lhsT(np.ascontiguousarray(inp["L0_attn_ffn_down_w"].T), 256),
        "b1_apxA": tile_lhsT(apxA, 128),
        "b1_apxB": tile_lhsT(apxB, 128),
    }


def core_x_slices(x):
    xf = np.ascontiguousarray(x.reshape(8192, 4096).T)
    outs = []
    for c in range(8):
        s = c // 2
        xs = xf[:, 2048 * s:2048 * (s + 1)]
        pad = np.zeros((4096, HALO), np.float32)
        padded = np.concatenate([pad, xs, pad], axis=1)
        if c % 2 == 0:
            outs.append(np.ascontiguousarray(padded[:, 0:TW]))
        else:
            outs.append(np.ascontiguousarray(padded[:, OWN:OWN + TW]))
    return outs


def build_b1(width=TW, stage="full"):
    nc = bass.Bass("TRN2", target_bir_lowering=False)
    io = {}
    io["xin"] = nc.dram_tensor("xin", [4096, width], F32, kind="ExternalInput").ap()
    io["xout"] = nc.dram_tensor("xout", [4096, width], F32, kind="ExternalOutput").ap()
    okind = "ExternalOutput" if stage in ("ffn", "down", "apxA", "apxB") else "Internal"
    io["b1_out"] = nc.dram_tensor("b1_out", [32, 128, width], BF16, kind=okind).ap()
    io["b1_ud"] = nc.dram_tensor("b1_ud", [2, 128, 8 * width], BF16, kind="Internal").ap()
    io["b1_h"] = nc.dram_tensor("b1_h", [78, 128, width], BF16,
                                kind="ExternalOutput" if stage == "ffn" else "Internal").ap()
    io["dbg"] = nc.dram_tensor("dbg", [KD, 128, width], BF16, kind="ExternalOutput").ap()
    io["dbg2"] = nc.dram_tensor("dbg2", [KD, 128, width], BF16, kind="ExternalOutput").ap()
    for nm, shape in (("b1_up", [8, 128, KD * 128]), ("b1_gate", [8, 128, KD * 128]),
                      ("b1_gw", [78, 128, KD * 128]), ("b1_uw", [78, 128, KD * 128]),
                      ("b1_dw", [16, 128, 78 * 256]), ("b1_apxA", [32, 128, 24 * 128]),
                      ("b1_apxB", [32, 128, 32 * 128]), ("b1_lp", [128, 8])):
        io[nm] = nc.dram_tensor(nm, shape, F32, kind="ExternalInput").ap()
    b = B(nc, width)
    b.emit_init()
    b.emit_block1(io, stage=stage)
    b.barrier(("sync",))
    with nc.Block() as block:
        b.build(block)
    return nc



def emit_block2(self, io, stage="full", pre_stats_done=False):
    W = self.W
    chs = chunks(W)
    xfm = io["xin"]
    xout = io["xout"]
    out_d = io["b_out"]
    NT = (W + 127) // 128

    self.mark()
    self.barrier(("gpsimd",))
    XLN = self.carve(BF16, (KD, W))
    wrb = self.carve(BF16, (KD, 64))
    rb = self.carve(F32, (1,))
    lgt = self.carve(F32, (NT, 64))
    sm = self.carve(F32, (NT, 64))
    wk = self.carve(F32, (64,))
    mxc = self.carve(F32, (4,))
    kp = self.carve(F32, (NT, 72))
    ktb = self.carve(BF16, (W,))
    wsr = self.carve(F32, (W,))
    ebb = self.carve(BF16, (4096,))
    nmrwb = self.carve(BF16, (W,))
    RWB = self.carve(F32, (W,))

    self.op_inc("gpsimd", lambda h, d=wrb.rearrange("p a b -> p (a b)"),
                s=io["b2_rw"]: h.dma_start(out=d, in_=s), "q12", 16)
    self.op_inc("gpsimd", lambda h, d=rb[0:64], s=io["b2_rb"]: h.dma_start(
        out=d, in_=s), "q13", 16)
    self.op_inc("gpsimd", lambda h, d=ebb[0:64], s=io["b2_eb"]: h.dma_start(
        out=d, in_=s), "q14", 16)

    def xtile(kt):
        return xfm[kt * 128:(kt + 1) * 128, :]

    if not pre_stats_done:
        self.emit_fm_stats(xtile, KD, W, True)
    self.finalize_stats(W)
    self.emit_ln_apply(xtile, KD, W, lambda kt: XLN[:, kt, :])

    sn = self.barrier(("tensor", "vector"))
    pv = None
    for ci, (o, wd) in enumerate(chs):
        for k in range(KD):
            fn = lambda h, p=self.PS[0:64, ci, 0:wd], l=wrb[:, k, :], \
                r=XLN[:, k, o:o + wd], st=(k == 0), sp_=(k == KD - 1): \
                h.matmul(p, l, r, start=st, stop=sp_)
            if ci == len(chs) - 1 and k == KD - 1:
                pv = self.op_inc("tensor", fn, "sp", 1)
            else:
                self.op("tensor", fn)
    self.w("vector", "sp", pv)
    lg = self.carve(F32, (W,))
    for ci, (o, wd) in enumerate(chs):
        self.op("vector", lambda h, d=lg[0:64, o:o + wd], s=self.PS[0:64, ci, 0:wd]:
                h.tensor_scalar(d, s, rb[0:64, 0:1], None, op0=ALU.add))
    vv = self.sems["sv"].n

    self.w("tensor", "sv", vv)
    for tt in range(NT):
        tw = min(128, W - tt * 128)
        tv = self.op_inc("tensor", lambda h, p=self.PS[0:tw, 3, 0:64],
                         s=lg[0:64, tt * 128:tt * 128 + tw], i=self.ident32[0:64, 0:64]:
                         h.transpose(p, s, i), "sp", 1)
        self.w("vector", "sp", tv)
        self.op("vector", lambda h, d=lgt[0:tw, tt, :], s=self.PS[0:tw, 3, 0:64]:
                h.tensor_copy(d, s))
        self.chain("vector", lambda h, d=mxc[0:tw, 0:1], s=lgt[0:tw, tt, :]:
                   h.reduce_max(d, s, axis=AX.X, negate=True))
        smv = self.sems["sv"].n
        self.w("scalar", "sv", smv)
        ev = self.op("scalar", lambda h, d=sm[0:tw, tt, :], s=lgt[0:tw, tt, :],
                     b=mxc[0:tw, 0:1]: h.activation(d, s, AF.Exp, bias=b))
        self.w("vector", "ss", ev)
        self.chain("vector", lambda h, d=mxc[0:tw, 1:2], s=sm[0:tw, tt, :]:
                   h.reduce_sum(d, s, axis=AX.X))
        self.chain("vector", lambda h, d=mxc[0:tw, 2:3], s=mxc[0:tw, 1:2]:
                   h.reciprocal(d, s))
        self.chain("vector", lambda h, d=sm[0:tw, tt, :], s=sm[0:tw, tt, :],
                   r=mxc[0:tw, 2:3]: h.tensor_scalar(d, s, r, None, op0=ALU.mult))
        self.chain("vector", lambda h, d=wk[0:tw, 0:64], s=sm[0:tw, tt, :]:
                   h.tensor_copy(d, s))
        for j in range(8):
            self.chain("vector", lambda h, d=kp[0:tw, tt, 70:71], s=wk[0:tw, 0:64]:
                       h.reduce_max(d, s, axis=AX.X))
            if j < 7:
                self.chain("vector", lambda h, d=lgt[0:tw, tt, :], s=wk[0:tw, 0:64],
                           m=kp[0:tw, tt, 70:71]: h.tensor_scalar(d, s, m, None, op0=ALU.is_ge))
                self.chain("vector", lambda h, d=wk[0:tw, 0:64], s=lgt[0:tw, tt, :],
                           w2=wk[0:tw, 0:64]: h.scalar_tensor_tensor(
                    d, s, -4.0, w2, op0=ALU.mult, op1=ALU.add))
        self.chain("vector", lambda h, d=lgt[0:tw, tt, :], s=sm[0:tw, tt, :],
                   m=kp[0:tw, tt, 70:71]: h.tensor_scalar(d, s, m, None, op0=ALU.is_ge))
        self.chain("vector", lambda h, d=kp[0:tw, tt, 0:64], s0=lgt[0:tw, tt, :],
                   s1=sm[0:tw, tt, :]: h.tensor_tensor(d, s0, s1, op=ALU.mult))
        kv = self.chain("vector", lambda h, d=kp[0:tw, tt, 64:65], s=kp[0:tw, tt, 0:64]:
                        h.reduce_sum(d, s, axis=AX.X))
        self.w("tensor", "sv", kv)
        tv2 = self.op_inc("tensor", lambda h, p=self.PS[0:65, 3, 0:tw],
                          s=kp[0:tw, tt, 0:65], i=self.ident32[0:tw, 0:tw]:
                          h.transpose(p, s, i), "sp", 1)
        self.w("scalar", "sp", tv2)
        self.op("scalar", lambda h, d=ktb[0:64, tt * 128:tt * 128 + tw],
                s=self.PS[0:64, 3, 0:tw]: h.copy(d, s))
        sv2 = self.op("scalar", lambda h, d=wsr[0:1, tt * 128:tt * 128 + tw],
                      s=self.PS[64:65, 3, 0:tw]: h.copy(d, s))
        self.w("tensor", "ss", sv2)

    self.barrier(("vector", "scalar", "tensor"))
    m_row = self.srow[:, 0, 0:W]
    r_row = self.srow[:, 1, 0:W]
    rw_row = self.srow[:, 2, 0:W]
    t_row = self.srow[:, 3, 0:W]
    self.op("vector", lambda h, d=rw_row, s0=r_row, s1=wsr[0:1, 0:W]:
            h.tensor_tensor(d, s0, s1, op=ALU.mult))
    self.chain("vector", lambda h, d=t_row, s0=m_row, s1=rw_row:
               h.tensor_tensor(d, s0, s1, op=ALU.mult))
    self.chain("vector", lambda h, d=nmrwb[0:1, 0:W], s=t_row: h.tensor_copy(d, s))
    bv = self.chain("vector", lambda h, d=t_row, s=rw_row: h.tensor_copy(d, s))
    self.w("tensor", "sv", bv)
    pv2 = None
    for ci, (o, wd) in enumerate(chs):
        fn = lambda h, p=self.PS[:, ci, 0:wd], r=rw_row[:, o:o + wd]: \
            h.matmul(p, self.ones_row32, r, start=True, stop=True)
        if ci == len(chs) - 1:
            pv2 = self.op_inc("tensor", fn, "sp", 1)
        else:
            self.op("tensor", fn)
    self.w("vector", "sp", pv2)
    for ci, (o, wd) in enumerate(chs):
        self.op("vector", lambda h, d=RWB[:, o:o + wd], s=self.PS[:, ci, 0:wd]:
                h.tensor_copy(d, s))

    self.barrier(("tensor", "vector", "sync", "scalar"))
    vdone = {}
    stv = {}
    for og in range(32):
        par = og % 2
        bank0 = par * 3
        slot = og % 2
        if og >= 2:
            self.w("tensor", "sv", vdone[og - 2])
        pv3 = None
        for ci, (o, wd) in enumerate(chs):
            self.op("tensor", lambda h, p=self.PS[:, bank0 + ci, 0:wd],
                    l=ebb[0:64, og * 128:(og + 1) * 128], r=ktb[0:64, o:o + wd]:
                    h.matmul(p, l, r, start=True, stop=False))
            pv3 = self.op_inc("tensor", lambda h, p=self.PS[:, bank0 + ci, 0:wd],
                              l=self.ones_row16[0:1, 0:128], r=nmrwb[0:1, o:o + wd]:
                              h.matmul(p, l, r, start=False, stop=True), "sp", 1)
        if og >= 2:
            self.w("sync", "sv", vdone[og - 2])
        xv = self.op_inc("sync", lambda h, d=self.ring_x[:, slot, 0:W], og=og:
                         h.dma_start(out=d, in_=xfm[og * 128:(og + 1) * 128, :]), f"q{slot}", 16)
        self.w("vector", f"q{slot}", xv)
        self.w("vector", "sp", pv3)
        if og >= 2:
            self.w("vector", f"q{8 + slot}", stv[og - 2])
        self.op("vector", lambda h, d=self.ring_c[:, slot, 0:W], s0=self.ring_x[:, slot, 0:W],
                s1=RWB[:, 0:W]: h.tensor_tensor(d, s0, s1, op=ALU.mult))
        vv3 = None
        for ci, (o, wd) in enumerate(chs):
            vv3 = self.chain("vector", lambda h, d=self.ring_a[:, slot, o:o + wd],
                             s0=self.ring_c[:, slot, o:o + wd], s1=self.PS[:, bank0 + ci, 0:wd]:
                             h.tensor_tensor(d, s0, s1, op=ALU.add))
        vdone[og] = vv3
        self.w("scalar", "sv", vv3)
        stv[og] = self.op_inc("scalar", lambda h, d=out_d[og], s=self.ring_a[:, slot, 0:W]:
                              h.dma_start(out=d, in_=s), f"q{8 + slot}", 16)
    self.release()

    otile = lambda kt: out_d[kt]
    self.emit_fm_stats(otile, KD, W, False)
    self.finalize_stats(W)
    self.emit_resid(xtile, otile, KD, W,
                    lambda kt: xout[kt * 128:(kt + 1) * 128, :], next_stats=True)


def prep_b2(inp):
    rw = np.ascontiguousarray(inp["L0_ffn_router_weight"].T)
    rwt = rw.reshape(KD, 128, 64).transpose(1, 0, 2).reshape(128, KD * 64)
    return {
        "b2_rw": np.ascontiguousarray(rwt),
        "b2_rb": np.ascontiguousarray(inp["L0_ffn_router_bias"].reshape(64, 1)),
        "b2_eb": np.ascontiguousarray(inp["L0_ffn_expert_bias"]),
    }


B.emit_block2 = emit_block2


def build_b2(width=TW, stage="full"):
    nc = bass.Bass("TRN2", target_bir_lowering=False)
    io = {}
    io["xin"] = nc.dram_tensor("xin", [4096, width], F32, kind="ExternalInput").ap()
    io["xout"] = nc.dram_tensor("xout", [4096, width], F32, kind="ExternalOutput").ap()
    io["b_out"] = nc.dram_tensor("b_out", [32, 128, width], BF16, kind="Internal").ap()
    io["b2_rw"] = nc.dram_tensor("b2_rw", [128, KD * 64], F32, kind="ExternalInput").ap()
    io["b2_rb"] = nc.dram_tensor("b2_rb", [64, 1], F32, kind="ExternalInput").ap()
    io["b2_eb"] = nc.dram_tensor("b2_eb", [64, 4096], F32, kind="ExternalInput").ap()
    b = B(nc, width)
    b.emit_init()
    b.emit_block2(io, stage=stage)
    b.barrier(("sync",))
    with nc.Block() as block:
        b.build(block)
    return nc


def emit_block3(self, io, stage="full", pre_stats_done=False):
    W = self.W
    WO = OWN
    cho = chunks(WO)
    xfm = io["xin"]
    xout = io["xout"]
    out_d = io["b_out"]

    self.mark()
    self.barrier(("gpsimd",))
    qc = self.carve_high(BF16, (8, WO))
    kc = self.carve_high(BF16, (8, WO))
    vc = self.carve_high(BF16, (8, WO))
    cb = self.carve(F32, (24,))
    bqf = self.carve(F32, (8,))
    bkf = self.carve(F32, (8,))
    bof = self.carve(F32, (8,))
    pbf = self.carve(F32, (32,))
    edge = self.carve(F32, (2,))
    self.op_inc("gpsimd", lambda h, d=cb, s=io["b3_cb"]: h.dma_start(out=d, in_=s), "q14", 16)
    self.op_inc("gpsimd", lambda h, d=bqf, s=io["b3_bq"]: h.dma_start(out=d, in_=s), "q14", 16)
    self.op_inc("gpsimd", lambda h, d=bkf, s=io["b3_bk"]: h.dma_start(out=d, in_=s), "q14", 16)
    self.op_inc("gpsimd", lambda h, d=bof, s=io["b3_bo"]: h.dma_start(out=d, in_=s), "q14", 16)
    self.op_inc("gpsimd", lambda h, d=pbf, s=io["b3_pb"]: h.dma_start(out=d, in_=s), "q14", 16)
    self.op_inc("gpsimd", lambda h, d=edge, s=io["b3_edge"]: h.dma_start(out=d, in_=s), "q14", 16)
    bvrow = self.carve(BF16, (1024,))
    self.op_inc("gpsimd", lambda h, d=bvrow[0:1], s=io["b3_bv"]: h.dma_start(out=d, in_=s), "q15", 16)

    def xtile(kt):
        return xfm[kt * 128:(kt + 1) * 128, :]

    self.mark()
    XLN = self.carve(BF16, (KD, W))
    if not pre_stats_done:
        self.emit_fm_stats(xtile, KD, W, True)
    self.finalize_stats(W)
    self.emit_ln_apply(xtile, KD, W, lambda kt: XLN[:, kt, :])
    self.barrier(("vector",))
    for k in range(KD):
        self.op("vector", lambda h, d=XLN[:, k, 15:16], m=edge[:, 0:1]:
                h.tensor_scalar(d, d, m, None, op0=ALU.mult))
        self.op("vector", lambda h, d=XLN[:, k, 1040:1041], m=edge[:, 1:2]:
                h.tensor_scalar(d, d, m, None, op0=ALU.mult))
    if stage == "ln":
        return
    self.mark()
    cwb = self.carve(BF16, (4, KD, 128))
    self.barrier(("gpsimd", "tensor", "scalar"))
    cW = io["b3_cw"]
    edone = {}
    jdone = {}
    for og in range(24):
        bank0 = (og % 2) * 2
        for j in range(3):
            gi = og * 3 + j
            slot = gi % 4
            prev = None if gi < 4 else ((gi - 4) // 3, (gi - 4) % 3)
            if prev in jdone:
                self.w("gpsimd", "sp", jdone[prev])
            gv = self.op_inc("gpsimd", lambda h, d=cwb[:, slot], og=og, j=j: h.dma_start(
                out=d, in_=cW[og, j]), f"q{12 + slot}", 16)
            self.w("tensor", f"q{12 + slot}", gv)
            if j == 0 and og >= 2:
                self.w("tensor", "ss", edone[og - 2])
            for k in range(KD):
                for ci, (o, wd) in enumerate(cho):
                    fn = lambda h, p=self.PS[:, bank0 + ci, 0:wd], \
                        l=cwb[:, slot, k, :], \
                        r=XLN[:, k, 15 + j + o:15 + j + o + wd], \
                        st=(j == 0 and k == 0), sp_=(j == 2 and k == KD - 1): \
                        h.matmul(p, l, r, start=st, stop=sp_)
                    if k == KD - 1 and ci == 1:
                        jdone[(og, j)] = self.op_inc("tensor", fn, "sp", 1)
                    else:
                        self.op("tensor", fn)
        self.w("scalar", "sp", jdone[(og, 2)])
        gt = og
        dstt = qc if gt < 8 else (kc if gt < 16 else vc)
        lt = gt % 8
        ev = self.op("scalar", lambda h, d=dstt[:, lt, 0:WO],
                     s=self.PS[:, bank0:bank0 + 2, :].rearrange("p a b -> p (a b)")[:, 0:WO],
                     gt=gt: h.activation(d, s, AF.Identity, bias=cb[:, gt:gt + 1]))
        edone[og] = ev
    self.release()
    self.release()

    if stage == "conv":
        return
    qhb = self.carve(BF16, (8, WO))
    khb = self.carve(BF16, (8, WO))
    vhb = self.carve(BF16, (8, 1024))
    self.mark()
    wbuf = self.carve(BF16, (2, 8, 256))
    self.barrier(("gpsimd", "tensor", "scalar"))

    def proj_phase(wdram, src, dstt, bias_fm, scale):
        self.barrier(("gpsimd", "tensor", "scalar"))
        evg = {}
        for og in range(4):
            slot = og % 2
            bank0 = (og % 2) * 4
            if og >= 2:
                self.w("gpsimd", "ss", evg[og - 2])
            gv = self.op_inc("gpsimd", lambda h, d=wbuf[:, slot], og=og: h.dma_start(
                out=d, in_=wdram[og]), f"q{12 + slot}", 16)
            self.w("tensor", f"q{12 + slot}", gv)
            if og >= 2:
                self.w("tensor", "ss", evg[og - 2])
            pv = None
            for ot in range(2):
                for ci, (o, wd) in enumerate(cho):
                    for k in range(8):
                        fn = lambda h, p=self.PS[:, bank0 + ot * 2 + ci, 0:wd], \
                            l=wbuf[:, slot, k, ot * 128:(ot + 1) * 128], \
                            r=src[:, k, o:o + wd], st=(k == 0), sp_=(k == 7): \
                            h.matmul(p, l, r, start=st, stop=sp_)
                        if ot == 1 and ci == 1 and k == 7:
                            pv = self.op_inc("tensor", fn, "sp", 1)
                        else:
                            self.op("tensor", fn)
            self.w("scalar", "sp", pv)
            ev = None
            for ot in range(2):
                gt = og * 2 + ot
                for ci, (o, wd) in enumerate(cho):
                    if scale is None:
                        ev = self.op("scalar", lambda h, d=dstt[:, gt, o:o + wd],
                                     s=self.PS[:, bank0 + ot * 2 + ci, 0:wd], gt=gt:
                                     h.activation(d, s, AF.Identity, bias=bias_fm[:, gt:gt + 1]))
                    else:
                        ev = self.op("scalar", lambda h, d=dstt[:, gt, o:o + wd],
                                     s=self.PS[:, bank0 + ot * 2 + ci, 0:wd], gt=gt:
                                     h.activation(d, s, AF.Identity, bias=bias_fm[:, gt:gt + 1],
                                                  scale=scale))
            evg[og] = ev
        return evg

    self.op("vector", lambda h, d=bqf: h.tensor_scalar(d, d, 0.125, None, op0=ALU.mult))
    self.barrier(("tensor", "gpsimd", "scalar"))
    proj_phase(io["b3_wq"], qc, qhb, bqf, 0.125)
    proj_phase(io["b3_wk"], kc, khb, bkf, None)

    wvb = self.carve(BF16, (8, 1024))
    gv2 = self.op_inc("gpsimd", lambda h, d=wvb.rearrange("p a b -> p (a b)"),
                      s=io["b3_wv"]: h.dma_start(out=d, in_=s), "q14", 16)
    self.barrier(("tensor", "scalar"))
    evt = {}
    for tt in range(8):
        par = tt % 2
        bank0 = par * 2
        if tt >= 2:
            self.w("tensor", "ss", evt[tt - 2])
        pv = None
        for ci in range(2):
            for k in range(8):
                fn = lambda h, p=self.PS[:, bank0 + ci, 0:512], \
                    l=vc[:, k, tt * 128:(tt + 1) * 128], \
                    r=wvb[:, k, ci * 512:(ci + 1) * 512], st=(k == 0): \
                    h.matmul(p, l, r, start=st, stop=False)
                self.op("tensor", fn)
            fn2 = lambda h, p=self.PS[:, bank0 + ci, 0:512], l=self.ones_row16[0:1, 0:128], \
                r=bvrow[0:1, ci * 512:(ci + 1) * 512]: h.matmul(p, l, r, start=False, stop=True)
            if ci == 1:
                pv = self.op_inc("tensor", fn2, "sp", 1)
            else:
                self.op("tensor", fn2)
        self.w("scalar", "sp", pv)
        ev = None
        for ci in range(2):
            ev = self.op("scalar", lambda h, d=vhb[:, tt, ci * 512:(ci + 1) * 512],
                         s=self.PS[:, bank0 + ci, 0:512]: h.copy(d, s))
        evt[tt] = ev
    self.release()
    self.release_high()

    if stage == "qkv":
        return
    sub = 9
    if stage.startswith("attn") and len(stage) > 4:
        sub = int(stage[4])
    sm = self.carve(F32, (16, 128))
    pb = self.carve(BF16, (16, 128))
    pTs = self.carve(BF16, (16, 128))
    mxb = self.carve(F32, (20,))
    ssb = self.carve(F32, (16,))
    oT = self.carve(BF16, (8, 1024))
    PSB = self.PS.bitcast(BF16)
    self.barrier(("tensor", "vector", "scalar"))
    prev_done = {}
    for b in range(8):
        if b >= 1:
            self.w("tensor", "sv", prev_done[b - 1])
        sv0 = None
        for hh in range(16):
            fn = lambda h, p=self.PS[:, hh % 4, (hh // 4) * 128:(hh // 4) * 128 + 128], \
                l=qhb[64 * (hh % 2):64 * (hh % 2) + 64, hh // 2, b * 128:(b + 1) * 128], \
                r=khb[64 * (hh % 2):64 * (hh % 2) + 64, hh // 2, b * 128:(b + 1) * 128]: \
                h.matmul(p, l, r, start=True, stop=True)
            if hh == 15:
                sv0 = self.op_inc("tensor", fn, "sp", 1)
            else:
                self.op("tensor", fn)
        if sub < 2:
            prev_done[b] = sv0 if False else None
            prev_done[b] = self.op("vector", lambda h, d=sm[:, 0, :], s0=self.PS[:, 0, 0:128], s1=self.causal: h.tensor_tensor(d, s0, s1, op=ALU.add))
            continue
        self.w("vector", "sp", sv0)
        for hh in range(16):
            self.op("vector", lambda h, d=sm[:, hh, :], s0=self.PS[:, hh % 4, (hh // 4) * 128:(hh // 4) * 128 + 128],
                    s1=self.causal: h.tensor_tensor(d, s0, s1, op=ALU.add))
        self.chain("vector", lambda h, d=mxb[:, 0:16], s=sm.rearrange("p a b -> p a b"):
                   h.reduce_max(d.rearrange("p (a o) -> p a o", o=1), s, axis=AX.X, negate=True))
        mv = self.sems["sv"].n
        self.w("scalar", "sv", mv)
        ev2 = None
        for hh in range(16):
            ev2 = self.op("scalar", lambda h, d=pb[:, hh, :], s=sm[:, hh, :],
                          bi=mxb[:, hh:hh + 1]: h.activation(d, s, AF.Exp, bias=bi))
        self.w("vector", "ss", ev2)
        self.chain("vector", lambda h, d=ssb[:, 0:16], s=pb:
                   h.reduce_sum(d.rearrange("p (a o) -> p a o", o=1), s, axis=AX.X))
        self.chain("vector", lambda h, d=mxb[:, 16:17] if False else ssb[:, 0:16], s=ssb[:, 0:16]:
                   h.reciprocal(d, s))
        for hh in range(16):
            self.chain("vector", lambda h, d=pb[:, hh, :], s=pb[:, hh, :],
                       r=ssb[:, hh:hh + 1]: h.tensor_scalar(d, s, r, None, op0=ALU.mult)) if hh == 0 else \
                self.op("vector", lambda h, d=pb[:, hh, :], s=pb[:, hh, :],
                        r=ssb[:, hh:hh + 1]: h.tensor_scalar(d, s, r, None, op0=ALU.mult))
        pnv = self.sems["sv"].n
        if sub < 3:
            prev_done[b] = pnv
            continue
        self.w("tensor", "sv", pnv)
        tv = None
        for hh in range(16):
            fn = lambda h, p=PSB[:, 4 + hh // 8, (hh % 8) * 128:(hh % 8) * 128 + 128], \
                s=pb[:, hh, :], i=self.ident16: h.transpose(p, s, i)
            if hh == 15:
                tv = self.op_inc("tensor", fn, "sp", 1)
            else:
                self.op("tensor", fn)
        self.w("scalar", "sp", tv)
        pev = None
        for hh in range(16):
            pev = self.op("scalar", lambda h, d=pTs[:, hh, :],
                          s=PSB[:, 4 + hh // 8, (hh % 8) * 128:(hh % 8) * 128 + 128]: h.copy(d, s))
        if sub < 4:
            prev_done[b] = self.op("vector", lambda h, d=oT[:, 0, b * 128:(b + 1) * 128], s=pTs[:, 0, :]: h.tensor_copy(d, s))
            continue
        self.w("tensor", "ss", pev)
        ov = None
        horder = list(range(0, 16, 2)) + list(range(1, 16, 2))
        for idx, hh in enumerate(horder):
            ht = hh // 2
            fn = lambda h, p=self.PS[64 * (hh % 2):64 * (hh % 2) + 64, 6 + ht // 4, (ht % 4) * 128:(ht % 4) * 128 + 128], \
                l=vhb[:, b, hh * 64:(hh + 1) * 64], r=pTs[:, hh, :]: \
                h.matmul(p, l, r, start=True, stop=True)
            if idx == 15:
                ov = self.op_inc("tensor", fn, "sp", 1)
            else:
                self.op("tensor", fn)
        self.w("vector", "sp", ov)
        done = None
        for ht in range(8):
            done = self.op("vector", lambda h, d=oT[:, ht, b * 128:(b + 1) * 128],
                           s=self.PS[:, 6 + ht // 4, (ht % 4) * 128:(ht % 4) * 128 + 128]:
                           h.tensor_copy(d, s))
        prev_done[b] = done
    if stage.startswith("attn"):
        self.release()
        return
    ao = self.carve(BF16, (8, WO))
    self.mark()
    wbuf2 = self.carve(BF16, (2, 8, 128))
    self.barrier(("gpsimd", "tensor", "scalar"))
    evg2 = {}
    for og in range(8):
        slot = og % 2
        bank0 = (og % 2) * 2
        if og >= 2:
            self.w("gpsimd", "ss", evg2[og - 2])
        gv = self.op_inc("gpsimd", lambda h, d=wbuf2[:, slot], og=og: h.dma_start(
            out=d, in_=io["b3_wo"][og]), f"q{12 + slot}", 16)
        self.w("tensor", f"q{12 + slot}", gv)
        if og >= 2:
            self.w("tensor", "ss", evg2[og - 2])
        pv = None
        for ci, (o, wd) in enumerate(cho):
            for k in range(8):
                fn = lambda h, p=self.PS[:, bank0 + ci, 0:wd], l=wbuf2[:, slot, k, :], \
                    r=oT[:, k, o:o + wd], st=(k == 0), sp_=(k == 7): \
                    h.matmul(p, l, r, start=st, stop=sp_)
                if ci == 1 and k == 7:
                    pv = self.op_inc("tensor", fn, "sp", 1)
                else:
                    self.op("tensor", fn)
        self.w("scalar", "sp", pv)
        ev = None
        for ci, (o, wd) in enumerate(cho):
            ev = self.op("scalar", lambda h, d=ao[:, og, o:o + wd],
                         s=self.PS[:, bank0 + ci, 0:wd], og=og: h.activation(
                d, s, AF.Identity, bias=bof[:, og:og + 1]))
        evg2[og] = ev
    self.release()

    self.mark()
    wbuf3 = self.carve(BF16, (2, 8, 128))
    self.barrier(("gpsimd", "tensor", "scalar"))
    evg3 = {}
    stv3 = {}
    for og in range(32):
        slot = og % 2
        bank0 = (og % 2) * 2
        if og >= 2:
            self.w("gpsimd", "ss", evg3[og - 2])
        gv = self.op_inc("gpsimd", lambda h, d=wbuf3[:, slot], og=og: h.dma_start(
            out=d, in_=io["b3_pw"][og]), f"q{12 + slot}", 16)
        self.w("tensor", f"q{12 + slot}", gv)
        if og >= 2:
            self.w("tensor", "ss", evg3[og - 2])
        pv = None
        for ci, (o, wd) in enumerate(cho):
            for k in range(8):
                fn = lambda h, p=self.PS[:, bank0 + ci, 0:wd], l=wbuf3[:, slot, k, :], \
                    r=ao[:, k, o:o + wd], st=(k == 0), sp_=(k == 7): \
                    h.matmul(p, l, r, start=st, stop=sp_)
                if ci == 1 and k == 7:
                    pv = self.op_inc("tensor", fn, "sp", 1)
                else:
                    self.op("tensor", fn)
        self.w("scalar", "sp", pv)
        if og >= 2:
            self.w("scalar", f"q{8 + slot}", stv3[og - 2])
        ev = None
        for ci, (o, wd) in enumerate(cho):
            ev = self.op("scalar", lambda h, d=self.ring_a[:, slot, o:o + wd],
                         s=self.PS[:, bank0 + ci, 0:wd], og=og: h.activation(
                d, s, AF.Identity, bias=pbf[:, og:og + 1]))
        evg3[og] = ev
        self.w("scalar", "ss", ev)
        stv3[og] = self.op_inc("scalar", lambda h, d=out_d[og][:, 0:WO],
                               s=self.ring_a[:, slot, 0:WO]: h.dma_start(out=d, in_=s),
                               f"q{8 + slot}", 16)
    self.release()
    self.release()

    otile = lambda kt: out_d[kt][:, 0:WO]
    xtile_o = lambda kt: xfm[kt * 128:(kt + 1) * 128, HALO:HALO + WO]
    self.emit_fm_stats(otile, KD, WO, False)
    self.finalize_stats(WO)
    self.emit_resid(xtile_o, otile, KD, WO,
                    lambda kt: xout[kt * 128:(kt + 1) * 128, :], next_stats=True)


def prep_b3(inp):
    cw = inp["L1_attn_conv_w"]
    cwt = np.empty((24, 3, 128, KD * 128), np.float32)
    for og in range(24):
        for j in range(3):
            blkw = cw[j][:, og * 128:(og + 1) * 128]
            cwt[og, j] = blkw.reshape(KD, 128, 128).transpose(1, 0, 2).reshape(128, KD * 128)
    wq = np.ascontiguousarray(inp["L1_attn_wq"].reshape(1024, 1024))
    wk = np.ascontiguousarray(inp["L1_attn_wk"].reshape(1024, 1024))
    wv = np.ascontiguousarray(inp["L1_attn_wv"].reshape(1024, 1024))
    wo = np.ascontiguousarray(inp["L1_attn_wo"].reshape(1024, 1024))
    pw = np.ascontiguousarray(inp["L1_attn_proj_w"])
    wvt = wv.reshape(8, 128, 1024).transpose(1, 0, 2).reshape(128, 8 * 1024)
    return {
        "b3_cw": np.ascontiguousarray(cwt),
        "b3_cb": np.ascontiguousarray(inp["L1_attn_conv_b"].reshape(24, 128).T),
        "b3_wq": tile_lhsT(wq, 256),
        "b3_wk": tile_lhsT(wk, 256),
        "b3_wv": np.ascontiguousarray(wvt),
        "b3_wo": tile_lhsT(wo, 128),
        "b3_pw": tile_lhsT(pw, 128),
        "b3_bq": np.ascontiguousarray(inp["L1_attn_bq"].reshape(-1).reshape(8, 128).T),
        "b3_bk": np.ascontiguousarray(inp["L1_attn_bk"].reshape(-1).reshape(8, 128).T),
        "b3_bv": np.ascontiguousarray(inp["L1_attn_bv"].reshape(1, 1024)),
        "b3_bo": np.ascontiguousarray(inp["L1_attn_bo"].reshape(8, 128).T),
        "b3_pb": np.ascontiguousarray(inp["L1_attn_proj_b"].reshape(32, 128).T),
    }


def build_b3(width=TW, stage="full"):
    nc = bass.Bass("TRN2", target_bir_lowering=False)
    io = {}
    io["xin"] = nc.dram_tensor("xin", [4096, width], F32, kind="ExternalInput").ap()
    io["xout"] = nc.dram_tensor("xout", [4096, OWN], F32, kind="ExternalOutput").ap()
    io["b_out"] = nc.dram_tensor("b_out", [32, 128, width], BF16, kind="Internal").ap()
    io["b3_cw"] = nc.dram_tensor("b3_cw", [24, 3, 128, KD * 128], F32, kind="ExternalInput").ap()
    io["b3_cb"] = nc.dram_tensor("b3_cb", [128, 24], F32, kind="ExternalInput").ap()
    io["b3_wq"] = nc.dram_tensor("b3_wq", [4, 128, 8 * 256], F32, kind="ExternalInput").ap()
    io["b3_wk"] = nc.dram_tensor("b3_wk", [4, 128, 8 * 256], F32, kind="ExternalInput").ap()
    io["b3_wv"] = nc.dram_tensor("b3_wv", [128, 8 * 1024], F32, kind="ExternalInput").ap()
    io["b3_wo"] = nc.dram_tensor("b3_wo", [8, 128, 8 * 128], F32, kind="ExternalInput").ap()
    io["b3_pw"] = nc.dram_tensor("b3_pw", [32, 128, 8 * 128], F32, kind="ExternalInput").ap()
    for nm, shp in (("b3_bq", [128, 8]), ("b3_bk", [128, 8]), ("b3_bo", [128, 8]),
                    ("b3_pb", [128, 32]), ("b3_bv", [1, 1024]), ("b3_edge", [128, 2])):
        io[nm] = nc.dram_tensor(nm, shp, F32, kind="ExternalInput").ap()
    b = B(nc, width)
    b.emit_init()
    b.emit_block3(io, stage=stage)
    b.barrier(("sync",))
    with nc.Block() as block:
        b.build(block)
    return nc


B.emit_block3 = emit_block3


def emit_block4(self, io, stage="full", pre_stats_done=False):
    WO = OWN
    cho = chunks(WO)
    xfm = io["xin"]
    xout = io["xout"]
    out_d = io["b_out"]
    xln_d = io["b4_xln"]

    self.mark()
    self.barrier(("gpsimd",))
    xe = self.carve_high(BF16, (16, WO))
    xo = self.carve_high(BF16, (16, WO))
    sc_fm = self.carve_high(F32, (32,))
    ZB = self.carve_high(BF16, (4096,))
    CL = self.carve_high(BF16, (WO,))
    apb = self.carve_high(BF16, (KD, 16))
    mz = self.op("vector", lambda h, d=ZB: h.memset(d, 0.0))
    self.w("gpsimd", "sv", mz)
    self.op_inc("gpsimd", lambda h, d=sc_fm, s=io["b4_sc"]: h.dma_start(out=d, in_=s), "q14", 16)
    self.op_inc("gpsimd", lambda h, d=ZB[0:1], s=io["b4_z"]: h.dma_start(out=d, in_=s), "q14", 16)
    self.op_inc("gpsimd", lambda h, d=ZB[32:48], s=io["b4_bt"]: h.dma_start(out=d, in_=s), "q14", 16)
    self.op_inc("gpsimd", lambda h, d=apb.rearrange("p a b -> p (a b)"), s=io["b4_at"]:
                h.dma_start(out=d, in_=s), "q15", 16)

    def xtile(kt):
        return xfm[kt * 128:(kt + 1) * 128, :]

    xsc = self.carve_high(BF16, (4, WO))
    if not pre_stats_done:
        self.emit_fm_stats(xtile, KD, WO, True)
    self.finalize_stats(WO)
    self.emit_ln_apply(xtile, KD, WO, lambda kt: xsc[:, kt % 4, :],
                       also_dram_fn=lambda kt: xln_d[kt * 128:(kt + 1) * 128, :])
    self.barrier(("sync", "tensor", "vector", "scalar"))
    for jt in range(16):
        self.op_inc("sync", lambda h, d=xe[:, jt, :], jt=jt: h.dma_start(
            out=d, in_=xln_d[jt * 256:(jt + 1) * 256:2, :]), f"q{jt % 2}", 16)
        self.op_inc("sync", lambda h, d=xo[:, jt, :], jt=jt: h.dma_start(
            out=d, in_=xln_d[jt * 256 + 1:(jt + 1) * 256:2, :]), f"q{2 + jt % 2}", 16)
    self.barrier(("tensor", "vector", "scalar"))

    def eo(k):
        return xe[:, k, :] if k < 16 else xo[:, k - 16, :]

    pv = None
    for ci, (o, wd) in enumerate(cho):
        for k in range(32):
            self.op("tensor", lambda h, p=self.PS[0:1, ci, 0:wd], r=eo(k)[:, o:o + wd],
                    st=(k == 0), sp_=(k == 31): h.matmul(p, self.ones_col, r, start=st, stop=sp_))
    for ci, (o, wd) in enumerate(cho):
        for k in range(32):
            fn = lambda h, p=self.PS[0:16, 2 + ci, 0:wd], l=apb[:, k, :], r=eo(k)[:, o:o + wd], \
                st=(k == 0), sp_=(k == 31): h.matmul(p, l, r, start=st, stop=sp_)
            if ci == 1 and k == 31:
                pv = self.op_inc("tensor", fn, "sp", 1)
            else:
                self.op("tensor", fn)
    self.w("scalar", "sp", pv)
    for ci, (o, wd) in enumerate(cho):
        self.op("scalar", lambda h, d=CL[0:1, o:o + wd], s=self.PS[0:1, ci, 0:wd]: h.copy(d, s))
    ev = None
    for ci, (o, wd) in enumerate(cho):
        ev = self.op("scalar", lambda h, d=CL[32:48, o:o + wd], s=self.PS[0:16, 2 + ci, 0:wd]:
                     h.activation(d, s, AF.Identity, scale=2.0))
    self.w("tensor", "ss", ev)

    qbuf = self.carve(U8, (2, 16, 128))
    y2b = self.carve(F32, (2, 2048))
    tb = self.carve(F32, (2, 2048))
    hbw = self.carve(BF16, (4, 16, 128))
    lbw = self.carve(BF16, (4, 16, 128))
    self.barrier(("sync", "gpsimd", "tensor", "vector", "scalar"))
    M23 = 12582912.0
    qW = io["b4_qw"]
    vdone = {}
    sdone = {}
    stv = {}
    pdone = {}
    edone = {}
    for pr in range(16):
        ogs = [2 * pr, 2 * pr + 1]
        for og in ogs:
            qs = og % 2
            if pr >= 1:
                self.w("sync", "ss", sdone[pr - 1])
            self.op_inc("sync", lambda h, d=qbuf[:, qs].rearrange("p a b -> p (a b)"),
                        og=og: h.dma_start(out=d, in_=qW[og]), f"q{4 + qs}", 16)
        for og in ogs:
            qs = og % 2
            self.w("scalar", f"q{4 + qs}", self.sems[f"q{4 + qs}"].n)
            if pr >= 1:
                self.w("scalar", "sv", vdone[pr - 1])
            sd = self.op("scalar", lambda h, d=y2b[:, qs, :],
                         s=qbuf[:, qs].rearrange("p a b -> p (a b)"):
                         h.activation(d, s, AF.Identity, scale=1.0 / 16.0))
        sdone[pr] = sd
        for og in ogs:
            qs = og % 2
            self.w("vector", "ss", sdone[pr] - len(ogs) + (og - ogs[0]) + 1)
            if pr >= 2:
                self.w("vector", "sp", pdone[pr - 2])
            self.op("vector", lambda h, d=tb[:, qs, :], s=y2b[:, qs, :]:
                    h.tensor_scalar(d, s, -0.46875, M23, op0=ALU.add, op1=ALU.add))
        for og in ogs:
            qs = og % 2
            slot = og % 4
            self.w("vector", "sv", self.sems["sv"].n - len(ogs) + (og - ogs[0]) + 1)
            self.op("vector", lambda h, d=hbw[:, slot].rearrange("p a b -> p (a b)"),
                    s=tb[:, qs, :]: h.tensor_scalar(d, s, -M23, None, op0=ALU.add))
        for og in ogs:
            qs = og % 2
            slot = og % 4
            self.w("vector", "sv", self.sems["sv"].n - len(ogs) + (og - ogs[0]) + 1)
            self.op("vector", lambda h, d=tb[:, qs, :], s0=y2b[:, qs, :],
                    s1=hbw[:, slot].rearrange("p a b -> p (a b)"): h.scalar_tensor_tensor(
                d, s0, 0.0, s1, op0=ALU.add, op1=ALU.subtract))
        vd = None
        for og in ogs:
            qs = og % 2
            slot = og % 4
            self.w("vector", "sv", self.sems["sv"].n - len(ogs) + (og - ogs[0]) + 1)
            vd = self.op("vector", lambda h, d=lbw[:, slot].rearrange("p a b -> p (a b)"),
                         s=tb[:, qs, :]: h.tensor_scalar(d, s, 16.0, None, op0=ALU.mult))
        vdone[pr] = vd
        self.w("tensor", "sv", vdone[pr])
        if pr >= 1:
            self.w("tensor", "sv", edone[pr - 1])
        pv2s = []
        for og in ogs:
            slot = og % 4
            bank0 = (og % 2) * 4
            pv2 = None
            for ci, (o, wd) in enumerate(cho):
                for k in range(32):
                    lh = (lambda h, p=self.PS[:, bank0 + ci, 0:wd],
                          l=(hbw if k < 16 else lbw)[:, slot, k % 16, :],
                          r=eo(k)[:, o:o + wd], st=(k == 0), sp_=(k == 31):
                          h.matmul(p, l, r, start=st, stop=sp_))
                    self.op("tensor", lh)
            for ci, (o, wd) in enumerate(cho):
                fn = lambda h, p=self.PS[:, bank0 + 2 + ci, 0:wd], l=ZB[0:48, og * 128:(og + 1) * 128], \
                    r=CL[0:48, o:o + wd]: h.matmul(p, l, r, start=True, stop=True)
                if ci == 1:
                    pv2 = self.op_inc("tensor", fn, "sp", 1)
                else:
                    self.op("tensor", fn)
            pv2s.append(pv2)
        pdone[pr] = pv2s[-1]
        ed = None
        for gi, og in enumerate(ogs):
            es = og % 2
            bank0 = (og % 2) * 4
            self.w("vector", "sp", pv2s[gi])
            if pr >= 1:
                self.w("vector", f"q{8 + es}", stv[(pr - 1, es)])
            for ci, (o, wd) in enumerate(cho):
                self.op("vector", lambda h, d=self.ring_c[:, es, o:o + wd],
                        s=self.PS[:, bank0 + ci, 0:wd], og=og: h.tensor_scalar(
                    d, s, sc_fm[:, og:og + 1], None, op0=ALU.mult))
                ed = self.chain("vector", lambda h, d=self.ring_a[:, es, o:o + wd],
                                s0=self.ring_c[:, es, o:o + wd], s1=self.PS[:, bank0 + 2 + ci, 0:wd]:
                                h.tensor_tensor(d, s0, s1, op=ALU.add))
            self.w("scalar", "sv", ed)
            stv[(pr, es)] = self.op_inc("scalar", lambda h, d=out_d[og][:, 0:WO],
                                        s=self.ring_a[:, es, 0:WO]: h.dma_start(out=d, in_=s),
                                        f"q{8 + es}", 16)
        edone[pr] = ed
    self.release_high()
    self.release()

    otile = lambda kt: out_d[kt][:, 0:WO]
    self.emit_fm_stats(otile, KD, WO, False)
    self.finalize_stats(WO)
    self.emit_resid(xtile, otile, KD, WO,
                    lambda kt: xout[kt * 128:(kt + 1) * 128, :])


def prep_b4(inp):
    qw = inp["L1_ffn_qweight"]
    qb = np.ascontiguousarray(qw.view(np.uint8).reshape(4096, 2048, 4)[:, :, 0])
    qbT = np.ascontiguousarray(qb.T)
    qt = np.empty((32, 128, 16 * 128), np.uint8)
    for og in range(32):
        sl = qbT[:, og * 128:(og + 1) * 128]
        qt[og] = sl.reshape(16, 128, 128).transpose(1, 0, 2).reshape(128, 16 * 128)
    At = np.ascontiguousarray(inp["L1_ffn_lora_A"].T)
    Ap = np.concatenate([At[0::2], At[1::2]], axis=0)
    apt = Ap.reshape(KD, 128, 16).transpose(1, 0, 2).reshape(128, KD * 16)
    return {
        "b4_qw": qt,
        "b4_sc": np.ascontiguousarray(inp["L1_ffn_scales"].astype(np.float32).reshape(32, 128).T),
        "b4_z": np.ascontiguousarray(inp["L1_ffn_zeros"].reshape(1, 4096)),
        "b4_bt": np.ascontiguousarray(inp["L1_ffn_lora_B"].T),
        "b4_at": np.ascontiguousarray(apt),
    }


def build_b4(stage="full"):
    nc = bass.Bass("TRN2", target_bir_lowering=False)
    io = {}
    io["xin"] = nc.dram_tensor("xin", [4096, OWN], F32, kind="ExternalInput").ap()
    io["xout"] = nc.dram_tensor("xout", [4096, OWN], F32, kind="ExternalOutput").ap()
    io["b_out"] = nc.dram_tensor("b_out", [32, 128, OWN], BF16, kind="Internal").ap()
    io["b4_xln"] = nc.dram_tensor("b4_xln", [4096, OWN], BF16, kind="Internal").ap()
    io["b4_qw"] = nc.dram_tensor("b4_qw", [32, 128, 16 * 128], U8, kind="ExternalInput").ap()
    io["b4_sc"] = nc.dram_tensor("b4_sc", [128, 32], F32, kind="ExternalInput").ap()
    io["b4_z"] = nc.dram_tensor("b4_z", [1, 4096], F16, kind="ExternalInput").ap()
    io["b4_bt"] = nc.dram_tensor("b4_bt", [16, 4096], F32, kind="ExternalInput").ap()
    io["b4_at"] = nc.dram_tensor("b4_at", [128, KD * 16], F32, kind="ExternalInput").ap()
    b = B(nc, OWN)
    b.emit_init()
    b.emit_block4(io, stage=stage)
    b.barrier(("sync",))
    with nc.Block() as block:
        b.build(block)
    return nc


B.emit_block4 = emit_block4


from concourse.bass_utils import run_bass_kernel_spmd


def build_fused():
    nc = bass.Bass("TRN2", target_bir_lowering=False)
    io = {}
    io["xin"] = nc.dram_tensor("xin", [4096, TW], F32, kind="ExternalInput").ap()
    io["xout"] = nc.dram_tensor("xout", [4096, OWN], F32, kind="ExternalOutput").ap()
    xq1 = nc.dram_tensor("xq1", [4096, TW], F32, kind="Internal").ap()
    xq2 = nc.dram_tensor("xq2", [4096, TW], F32, kind="Internal").ap()
    xq3 = nc.dram_tensor("xq3", [4096, OWN], F32, kind="Internal").ap()
    io["b_out"] = nc.dram_tensor("b_out", [32, 128, TW], BF16, kind="Internal").ap()
    io["b1_out"] = io["b_out"]
    io["b1_h"] = nc.dram_tensor("b1_h", [78, 128, TW], BF16, kind="Internal").ap()
    io["b1_ud"] = nc.dram_tensor("b1_ud", [2, 128, 8 * TW], BF16, kind="Internal").ap()
    io["b4_xln"] = nc.dram_tensor("b4_xln", [4096, OWN], BF16, kind="Internal").ap()
    for nm, shape, dt_ in (("b1_up", [8, 128, KD * 128], F32), ("b1_gate", [8, 128, KD * 128], F32),
                           ("b1_gw", [78, 128, KD * 128], F32), ("b1_uw", [78, 128, KD * 128], F32),
                           ("b1_dw", [16, 128, 78 * 256], F32), ("b1_apxA", [32, 128, 24 * 128], F32),
                           ("b1_apxB", [32, 128, 32 * 128], F32), ("b1_lp", [128, 8], F32),
                           ("b2_rw", [128, KD * 64], F32), ("b2_rb", [64, 1], F32),
                           ("b2_eb", [64, 4096], F32),
                           ("b3_cw", [24, 3, 128, KD * 128], F32), ("b3_cb", [128, 24], F32),
                           ("b3_wq", [4, 128, 8 * 256], F32), ("b3_wk", [4, 128, 8 * 256], F32),
                           ("b3_wv", [128, 8 * 1024], F32), ("b3_wo", [8, 128, 8 * 128], F32),
                           ("b3_pw", [32, 128, 8 * 128], F32),
                           ("b3_bq", [128, 8], F32), ("b3_bk", [128, 8], F32),
                           ("b3_bo", [128, 8], F32), ("b3_pb", [128, 32], F32),
                           ("b3_bv", [1, 1024], F32), ("b3_edge", [128, 2], F32),
                           ("b4_qw", [32, 128, 16 * 128], U8), ("b4_sc", [128, 32], F32),
                           ("b4_z", [1, 4096], F16), ("b4_bt", [16, 4096], F32),
                           ("b4_at", [128, KD * 16], F32)):
        io[nm] = nc.dram_tensor(nm, shape, dt_, kind="ExternalInput").ap()

    b = B(nc, TW)
    b.emit_init()
    io1 = dict(io); io1["xin"] = io["xin"]; io1["xout"] = xq1
    b.emit_block1(io1)
    io2 = dict(io); io2["xin"] = xq1; io2["xout"] = xq2
    b.emit_block2(io2, pre_stats_done=True)
    io3 = dict(io); io3["xin"] = xq2; io3["xout"] = xq3
    b.emit_block3(io3, pre_stats_done=True)
    io4 = dict(io); io4["xin"] = xq3; io4["xout"] = io["xout"]
    b.emit_block4(io4, pre_stats_done=True)
    b.barrier(("sync",))
    with nc.Block() as block:
        b.build(block)
    return nc


def kernel(**inputs):
    inputs = {k: np.asarray(v) for k, v in inputs.items()}
    w = {}
    w.update(prep_b1(inputs))
    w.update(prep_b2(inputs))
    w.update(prep_b3(inputs))
    w.update(prep_b4(inputs))
    x = np.ascontiguousarray(inputs["L0_attn_hidden_states"].astype(np.float32, copy=False))
    xs = core_x_slices(x)
    in_maps = []
    for c in range(8):
        e = np.zeros((128, 2), np.float32)
        e[:, 0] = 1.0 if c % 2 == 1 else 0.0
        e[:, 1] = 1.0 if c % 2 == 0 else 0.0
        m = dict(w)
        m["xin"] = xs[c]
        m["b3_edge"] = e
        in_maps.append(m)
    nc = build_fused()
    res = run_bass_kernel_spmd(nc, in_maps, core_ids=list(range(8)))
    cur = [res.results[c]["xout"] for c in range(8)]
    full = np.concatenate(cur, axis=1)
    return np.ascontiguousarray(full.T).reshape(4, 2048, 4096).astype(np.float32, copy=False)


# version 09
# speedup vs baseline: 1.0324x; 1.0324x over previous
import numpy as np
import concourse.bass as bass
import concourse.mybir as mybir

F32 = mybir.dt.float32
BF16 = mybir.dt.bfloat16
U8 = mybir.dt.uint8
F16 = mybir.dt.float16
AF = mybir.ActivationFunctionType
ALU = mybir.AluOpType
AX = mybir.AxisListType

D = 4096
KD = 32
OWN = 1024
HALO = 16
TW = OWN + 2 * HALO
EPS = 1e-5
ENGINES = ("sync", "tensor", "vector", "scalar", "gpsimd")
ARENA_BYTES = 155 * 1024


def chunks(width, maxw=512):
    out = []
    o = 0
    while o < width:
        w = min(maxw, width - o)
        out.append((o, w))
        o += w
    return out


def tile_lhsT(w_km, ogm):
    K, M = w_km.shape
    KT = K // 128
    OG = M // ogm
    return np.ascontiguousarray(
        w_km.reshape(KT, 128, OG, ogm).transpose(2, 1, 0, 3).reshape(OG, 128, KT * ogm))


class Sem:
    def __init__(self, h):
        self.h = h
        self.n = 0

    def bump(self, k):
        self.n += k
        return self.n


class B:
    def __init__(self, nc, width):
        self.nc = nc
        self.W = width
        self.ops = {e: [] for e in ENGINES}
        self.sems = {}
        for name in ("sp", "sv", "ss", "sc") + tuple(f"q{i}" for i in range(16)):
            self.sems[name] = Sem(nc.alloc_semaphore(name))
        self.fixed()

    ESEM = {"vector": "sv", "scalar": "ss", "gpsimd": "sc"}

    def op(self, eng, fn):
        es = self.ESEM.get(eng)
        if es is not None:
            return self.op_inc(eng, fn, es, 1)
        self.ops[eng].append(fn)
        return None

    CHAIN_WAIT = {"vector": True, "scalar": True, "gpsimd": True}

    def chain(self, eng, fn):
        if self.CHAIN_WAIT.get(eng, True):
            es = self.ESEM[eng]
            self.w(eng, es, self.sems[es].n)
        return self.op(eng, fn)

    def w(self, eng, sem, val):
        if val and val > 0:
            self.ops[eng].append(lambda h, s=self.sems[sem].h, v=val: h.wait_ge(s, v))

    def op_inc(self, eng, fn, sem, k=1):
        s = self.sems[sem]

        def g(h, fn=fn, hh=s.h, k=k):
            fn(h).then_inc(hh, k)
        self.ops[eng].append(g)
        return s.bump(k)

    def snap(self):
        return {k: s.n for k, s in self.sems.items()}

    def barrier(self, engs, snap=None):
        snap = snap or self.snap()
        for e in engs:
            for k, v in snap.items():
                self.w(e, k, v)
        return snap

    def build(self, block):
        for eng in ENGINES:
            fns = self.ops[eng]
            section = getattr(block, eng)

            def body(h, fns=fns):
                for f in fns:
                    f(h)
            section(body)

    def fixed(self):
        nc = self.nc
        W = self.W
        self.PS = nc.alloc_psum_tensor("PS", [128, 8, 512], F32).ap()
        self.MBs = nc.alloc_sbuf_tensor("MBs", [128, W], F32).ap()
        self.RBs = nc.alloc_sbuf_tensor("RBs", [128, W], F32).ap()
        self.ring_x = nc.alloc_sbuf_tensor("ring_x", [128, 2, W], F32).ap()
        self.ring_c = nc.alloc_sbuf_tensor("ring_c", [128, 2, W], F32).ap()
        self.ring_a = nc.alloc_sbuf_tensor("ring_a", [128, 2, W], BF16).ap()
        self.ring_b = nc.alloc_sbuf_tensor("ring_b", [128, 2, W], BF16).ap()
        self.ones_col = nc.alloc_sbuf_tensor("ones_col", [128, 1], BF16).ap()
        self.ones_row32 = nc.alloc_sbuf_tensor("ones_row32", [1, 128], F32).ap()
        self.ones_row16 = nc.alloc_sbuf_tensor("ones_row16", [1, 512], BF16).ap()
        self.ident16 = nc.alloc_sbuf_tensor("ident16", [128, 128], BF16).ap()
        self.ident32 = nc.alloc_sbuf_tensor("ident32", [128, 128], F32).ap()
        self.causal = nc.alloc_sbuf_tensor("causal", [128, 128], F32).ap()
        self.srow = nc.alloc_sbuf_tensor("srow", [1, 4, TW], F32).ap()
        self.arena_t = nc.alloc_sbuf_tensor("AR", [128, ARENA_BYTES], U8)
        self.ar_off = 0
        self.ar_marks = []
        self.ar_hi = ARENA_BYTES

    def carve_high(self, dtype, shape):
        esz = mybir.dt.size(dtype)
        n = int(np.prod(shape)) * esz
        off = (self.ar_hi - n) // 512 * 512
        assert off >= self.ar_off, f"arena high overflow {off} < {self.ar_off}"
        self.ar_hi = off
        ap = self.arena_t.ap()[:, off:off + n].bitcast(dtype)
        if len(shape) > 1:
            spec = " ".join(f"a{i}" for i in range(len(shape)))
            kw = {f"a{i}": int(shape[i]) for i in range(len(shape))}
            ap = ap.rearrange(f"p ({spec}) -> p {spec}", **kw)
        return ap

    def release_high(self):
        self.ar_hi = ARENA_BYTES

    def carve(self, dtype, shape):
        esz = mybir.dt.size(dtype)
        n = int(np.prod(shape)) * esz
        off = (self.ar_off + 511) // 512 * 512
        assert off + n <= self.ar_hi, f"arena overflow {off}+{n} hi={self.ar_hi}"
        self.ar_off = off + n
        ap = self.arena_t.ap()[:, off:off + n].bitcast(dtype)
        if len(shape) > 1:
            spec = " ".join(f"a{i}" for i in range(len(shape)))
            kw = {f"a{i}": int(shape[i]) for i in range(len(shape))}
            ap = ap.rearrange(f"p ({spec}) -> p {spec}", **kw)
        return ap

    def mark(self):
        self.ar_marks.append(self.ar_off)

    def release(self):
        self.ar_off = self.ar_marks.pop()

    def emit_init(self):
        v = None
        for a, val in ((self.ones_col, 1.0), (self.ones_row32, 1.0), (self.ones_row16, 1.0),
                       (self.ident16, 1.0), (self.ident32, 1.0), (self.causal, 0.0)):
            v = self.op_inc("vector", lambda h, a=a, vv=val: h.memset(a, vv), "sv", 1)
        self.w("gpsimd", "sv", v)
        g = "gpsimd"
        self.op(g, lambda h, a=self.ident16: h.affine_select(
            a, a, pattern=[[1, 128]], base=0, channel_multiplier=-1,
            compare_op=ALU.is_equal, fill=0.0))
        self.op(g, lambda h, a=self.ident32: h.affine_select(
            a, a, pattern=[[1, 128]], base=0, channel_multiplier=-1,
            compare_op=ALU.is_equal, fill=0.0))
        self.op_inc(g, lambda h, a=self.causal: h.affine_select(
            a, a, pattern=[[-1, 128]], base=0, channel_multiplier=1,
            compare_op=ALU.is_ge, fill=-1e9), "sc", 1)
        self.barrier(ENGINES)

    def emit_fm_stats(self, tile_fn, n_kt, width, src_f32):
        self.barrier(("sync", "scalar", "tensor"))
        chs = chunks(width)
        G = 4
        self.mark()
        rx = self.carve(F32, (G, width)) if src_f32 else None
        ra = self.carve(BF16, (G, width))
        rb = self.carve(BF16, (G, width))
        done = {}
        sdone = {}
        for g0 in range(0, n_kt, G):
            grp = list(range(g0, min(g0 + G, n_kt)))
            for kt in grp:
                slot = kt % G
                if kt >= G:
                    self.w("sync", "ss", sdone[kt - G])
                    self.w("sync", "sp", done[kt - G])
                dst = rx[:, slot, :] if src_f32 else ra[:, slot, :]
                self.op_inc("sync", lambda h, d=dst, kt=kt: h.dma_start(
                    out=d, in_=tile_fn(kt)), f"q{slot}", 16)
            for kt in grp:
                slot = kt % G
                if kt >= G:
                    self.w("scalar", "sp", done[kt - G])
                self.w("scalar", f"q{slot}", self.sems[f"q{slot}"].n)
                if src_f32:
                    self.op("scalar", lambda h, o=ra[:, slot, :], i=rx[:, slot, :]: h.copy(o, i))
            for kt in grp:
                slot = kt % G
                xa = ra[:, slot, :]
                if src_f32:
                    self.w("scalar", "ss", self.sems["ss"].n - len(grp) + (kt - g0) + 1)
                sdone[kt] = self.op("scalar", lambda h, o=rb[:, slot, :], i=xa: h.square(o, i))
            for kt in grp:
                slot = kt % G
                self.w("tensor", "ss", sdone[kt])
                xa = ra[:, slot, :]
                xsq = rb[:, slot, :]
                for ci, (o, wd) in enumerate(chs):
                    self.op("tensor", lambda h, p=self.PS[0:1, ci, 0:wd], r=xa[:, o:o + wd],
                            st=(kt == 0), sp_=(kt == n_kt - 1):
                            h.matmul(p, self.ones_col, r, start=st, stop=sp_))
                for ci, (o, wd) in enumerate(chs):
                    fn = lambda h, p=self.PS[0:1, 3 + ci, 0:wd], r=xsq[:, o:o + wd], \
                        st=(kt == 0), sp_=(kt == n_kt - 1): \
                        h.matmul(p, self.ones_col, r, start=st, stop=sp_)
                    if ci == len(chs) - 1:
                        done[kt] = self.op_inc("tensor", fn, "sp", 1)
                    else:
                        self.op("tensor", fn)
        self.release()

    def finalize_stats(self, width):
        chs = chunks(width)
        self.barrier(("vector",))
        m_row = self.srow[:, 0, 0:width]
        v_row = self.srow[:, 1, 0:width]
        t_row = self.srow[:, 3, 0:width]
        inv = 1.0 / 4096.0
        for ci, (o, wd) in enumerate(chs):
            self.op("vector", lambda h, d=m_row[:, o:o + wd], s=self.PS[0:1, ci, 0:wd]:
                    h.tensor_scalar(d, s, inv, None, op0=ALU.mult))
            self.op("vector", lambda h, d=v_row[:, o:o + wd], s=self.PS[0:1, 3 + ci, 0:wd]:
                    h.tensor_scalar(d, s, inv, None, op0=ALU.mult))
        self.chain("vector", lambda h, d=t_row, s=m_row: h.tensor_tensor(d, s, s, op=ALU.mult))
        self.chain("vector", lambda h, d=v_row, s0=v_row, s1=t_row: h.tensor_tensor(d, s0, s1, op=ALU.subtract))
        vv = self.chain("vector", lambda h, d=v_row, s=v_row: h.tensor_scalar(
            d, s, EPS, None, op0=ALU.add))
        self.w("scalar", "sv", vv)
        sq = self.op("scalar", lambda h, d=t_row, s=v_row: h.sqrt(d, s))
        self.w("vector", "ss", sq)
        self.op("vector", lambda h, d=v_row, s=t_row: h.reciprocal(d, s))
        nm = self.chain("vector", lambda h, d=m_row, s=m_row: h.tensor_scalar(
            d, s, -1.0, None, op0=ALU.mult))
        self.w("tensor", "sv", nm)
        for ci, (o, wd) in enumerate(chs):
            self.op("tensor", lambda h, p=self.PS[:, ci, 0:wd], r=m_row[:, o:o + wd]:
                    h.matmul(p, self.ones_row32, r, start=True, stop=True))
        pv = None
        for ci, (o, wd) in enumerate(chs):
            fn = lambda h, p=self.PS[:, 3 + ci, 0:wd], r=v_row[:, o:o + wd]: \
                h.matmul(p, self.ones_row32, r, start=True, stop=True)
            if ci == len(chs) - 1:
                pv = self.op_inc("tensor", fn, "sp", 1)
            else:
                self.op("tensor", fn)
        self.w("vector", "sp", pv)
        for ci, (o, wd) in enumerate(chs):
            self.op("vector", lambda h, d=self.MBs[:, o:o + wd], s=self.PS[:, ci, 0:wd]:
                    h.tensor_copy(d, s))
        for ci, (o, wd) in enumerate(chs):
            self.op("vector", lambda h, d=self.RBs[:, o:o + wd], s=self.PS[:, 3 + ci, 0:wd]:
                        h.tensor_copy(d, s))

    def emit_ln_apply(self, tile_fn, n_kt, width, dst_fn, also_dram_fn=None):
        self.barrier(("sync", "vector", "gpsimd"))
        G = 4
        self.mark()
        rx = self.carve(F32, (G, width))
        rc = self.carve(F32, (G, width))

        def teng(kt):
            return "vector"

        vdone = {}
        adv = {}
        for g0 in range(0, n_kt, G):
            grp = list(range(g0, min(g0 + G, n_kt)))
            for kt in grp:
                slot = kt % G
                if kt >= G:
                    sem, val = vdone[kt - G]
                    self.w("sync", sem, val)
                self.op_inc("sync", lambda h, d=rx[:, slot, :], kt=kt: h.dma_start(
                    out=d, in_=tile_fn(kt)), f"q{slot}", 16)
            a1v = {}
            for kt in grp:
                slot = kt % G
                e = teng(kt)
                self.w(e, f"q{slot}", self.sems[f"q{slot}"].n)
                a1v[kt] = self.op(e, lambda h, d=rc[:, slot, :], s=rx[:, slot, :]: h.tensor_tensor(
                    d, s, self.MBs[:, 0:width], op=ALU.add))
            for kt in grp:
                slot = kt % G
                e = teng(kt)
                if also_dram_fn is not None and kt >= 4:
                    self.w(e, f"q{8 + kt % 4}", adv[kt - 4])
                self.w(e, self.ESEM[e], a1v[kt])
                v = self.op(e, lambda h, d=dst_fn(kt), s=rc[:, slot, :]: h.tensor_tensor(
                    d, s, self.RBs[:, 0:width], op=ALU.mult))
                vdone[kt] = (self.ESEM[e], v)
            if also_dram_fn is not None:
                for kt in grp:
                    sem, val = vdone[kt]
                    self.w("scalar", sem, val)
                    adv[kt] = self.op_inc("scalar", lambda h, d=also_dram_fn(kt), s=dst_fn(kt): h.dma_start(
                        out=d, in_=s), f"q{8 + kt % 4}", 16)
        self.release()

    def emit_resid(self, x_tile_fn, out_tile_fn, n_kt, width, dst_tile_fn,
                   next_stats=False):
        self.barrier(("sync", "vector", "scalar", "tensor", "gpsimd"))
        chs = chunks(width)
        G = 4
        self.mark()
        rx = self.carve(F32, (G, width))
        ra = self.carve(BF16, (G, width))
        rc = self.carve(F32, (G, width))
        rbq = self.carve(BF16, (G, width)) if next_stats else None
        rsq = self.carve(BF16, (G, width)) if next_stats else None

        def teng(kt):
            return "vector"

        vdone = {}
        stv = {}
        sdone = {}
        tdone = {}
        for g0 in range(0, n_kt, G):
            grp = list(range(g0, min(g0 + G, n_kt)))
            for kt in grp:
                slot = kt % G
                if kt >= G:
                    sem, val = vdone[kt - G]
                    self.w("sync", sem, val)
                    self.w("sync", f"q{8 + kt % 4}", stv[kt - G])
                    if next_stats:
                        self.w("sync", "ss", sdone[kt - G])
                self.op_inc("sync", lambda h, d=rx[:, slot, :], kt=kt: h.dma_start(
                    out=d, in_=x_tile_fn(kt)), f"q{slot}", 16)
            for kt in grp:
                slot = kt % G
                if kt >= G and next_stats:
                    self.w("sync", "sp", tdone[kt - G])
                self.op_inc("sync", lambda h, d=ra[:, slot, :], kt=kt: h.dma_start(
                    out=d, in_=out_tile_fn(kt)), f"q{4 + slot}", 16)
            a1v = {}
            a2v = {}
            for kt in grp:
                slot = kt % G
                e = teng(kt)
                self.w(e, f"q{slot}", self.sems[f"q{slot}"].n)
                self.w(e, f"q{4 + slot}", self.sems[f"q{4 + slot}"].n)
                a1v[kt] = self.op(e, lambda h, d=rc[:, slot, :], s=ra[:, slot, :]: h.tensor_tensor(
                    d, s, self.MBs[:, 0:width], op=ALU.add))
            for kt in grp:
                slot = kt % G
                e = teng(kt)
                self.w(e, self.ESEM[e], a1v[kt])
                a2v[kt] = self.op(e, lambda h, d=rc[:, slot, :], s=rc[:, slot, :]: h.tensor_tensor(
                    d, s, self.RBs[:, 0:width], op=ALU.mult))
            for kt in grp:
                slot = kt % G
                e = teng(kt)
                self.w(e, self.ESEM[e], a2v[kt])
                v = self.op(e, lambda h, d=rx[:, slot, :], s=rc[:, slot, :],
                            x2=rx[:, slot, :]: h.tensor_tensor(d, s, x2, op=ALU.add))
                vdone[kt] = (self.ESEM[e], v)
            for kt in grp:
                slot = kt % G
                sem, val = vdone[kt]
                self.w("sync", sem, val)
                stv[kt] = self.op_inc("sync", lambda h, d=dst_tile_fn(kt), s=rx[:, slot, :]: h.dma_start(
                    out=d, in_=s), f"q{8 + slot}", 16)
            if next_stats:
                for kt in grp:
                    slot = kt % G
                    sem, val = vdone[kt]
                    self.w("scalar", sem, val)
                    if kt >= G:
                        self.w("scalar", "sp", tdone[kt - G])
                    self.op("scalar", lambda h, o=rbq[:, slot, :], i=rx[:, slot, :]: h.copy(o, i))
                for kt in grp:
                    slot = kt % G
                    self.w("scalar", "ss", self.sems["ss"].n - len(grp) + (kt - g0) + 1)
                    sdone[kt] = self.op("scalar", lambda h, o=rsq[:, slot, :], i=rbq[:, slot, :]: h.square(o, i))
                for kt in grp:
                    slot = kt % G
                    self.w("tensor", "ss", sdone[kt])
                    for ci, (o, wd) in enumerate(chs):
                        self.op("tensor", lambda h, p=self.PS[0:1, ci, 0:wd], r=rbq[:, slot, o:o + wd],
                                st=(kt == 0), sp_=(kt == n_kt - 1):
                                h.matmul(p, self.ones_col, r, start=st, stop=sp_))
                    for ci, (o, wd) in enumerate(chs):
                        fn = lambda h, p=self.PS[0:1, 3 + ci, 0:wd], r=rsq[:, slot, o:o + wd], \
                            st=(kt == 0), sp_=(kt == n_kt - 1): \
                            h.matmul(p, self.ones_col, r, start=st, stop=sp_)
                        if ci == len(chs) - 1:
                            tdone[kt] = self.op_inc("tensor", fn, "sp", 1)
                        else:
                            self.op("tensor", fn)
        self.release()

    def stream_mm(self, w_dram, OG, KT, rhs_fn, evac_fn, width):
        chs = chunks(width)
        self.mark()
        wbuf = self.carve(BF16, (2, KT, 128))
        self.barrier(("gpsimd", "tensor"))
        evac_gate = {}
        for og in range(OG):
            slot = og % 2
            bank0 = (og % 2) * 3
            if og >= 2:
                g = evac_gate[og - 2]
                self.w("gpsimd", g["sem"], g["val"])
            gv = self.op_inc("gpsimd", lambda h, d=wbuf[:, slot], og=og: h.dma_start(
                out=d, in_=w_dram[og]), f"q{12 + slot}", 16)
            self.w("tensor", f"q{12 + slot}", gv)
            if og >= 2:
                g = evac_gate[og - 2]
                self.w("tensor", g["sem"], g["val"])
            pv = None
            for ci, (o, wd) in enumerate(chs):
                for k in range(KT):
                    fn = lambda h, p=self.PS[:, bank0 + ci, 0:wd], l=wbuf[:, slot, k, :], \
                        r=rhs_fn(k, o, wd), st=(k == 0), sp_=(k == KT - 1): \
                        h.matmul(p, l, r, start=st, stop=sp_)
                    if ci == len(chs) - 1 and k == KT - 1:
                        pv = self.op_inc("tensor", fn, "sp", 1)
                    else:
                        self.op("tensor", fn)
            evac_gate[og] = evac_fn(og, bank0, pv)
        self.release()

    def emit_block1(self, io, stage="full", pre_stats_done=False):
        W = self.W
        chs = chunks(W)
        xfm = io["xin"]
        xout = io["xout"]
        out_d = io["b1_out"]
        h_d = io["b1_h"]
        fl = lambda a: a.rearrange("p a b -> p (a b)")

        NH = 22
        self.mark()
        lp = self.carve(F32, (8,))
        gv = self.op_inc("gpsimd", lambda h, d=lp: h.dma_start(out=d, in_=io["b1_lp"]), "q12", 16)

        def xtile(kt):
            return xfm[kt * 128:(kt + 1) * 128, :]

        self.mark()
        XLN = self.carve(BF16, (KD, W))
        self.mark()
        U = self.carve(BF16, (8, W))
        delta = self.carve(BF16, (8, W))
        if not pre_stats_done:
            self.emit_fm_stats(xtile, KD, W, True)
        self.finalize_stats(W)
        self.emit_ln_apply(xtile, KD, W, lambda kt: XLN[:, kt, :])
        xln_rhs = lambda k, o, wd: XLN[:, k, o:o + wd]
        if stage == "ln":
            self.dump_sbuf(XLN, io["dbg"], (KD, W))
            return

        def evac_U(og, bank0, done):
            self.w("scalar", "sp", done)
            v = None
            for ci, (o, wd) in enumerate(chs):
                v = self.op_inc("scalar", lambda h, d=U[:, og, o:o + wd],
                                s=self.PS[:, bank0 + ci, 0:wd]: h.copy(d, s), "ss", 1)
            return {"sem": "ss", "val": v}

        def evac_G(og, bank0, done):
            self.w("vector", "sp", done)
            self.w("vector", "q12", gv)
            v = None
            for ci, (o, wd) in enumerate(chs):
                v = self.op_inc("vector", lambda h, d=delta[:, og, o:o + wd],
                                s=self.PS[:, bank0 + ci, 0:wd], og=og: h.tensor_scalar(
                    d, s, lp[:, og:og + 1], None, op0=ALU.subtract), "sv", 1)
            return {"sem": "sv", "val": v}

        self.stream_mm(io["b1_up"], 8, KD, xln_rhs, evac_U, W)
        self.stream_mm(io["b1_gate"], 8, KD, xln_rhs, evac_G, W)
        if stage == "ug":
            self.dump_sbuf(U, io["dbg"], (8, W))
            self.dump_sbuf(delta, io["dbg2"], (8, W))
            return
        self.barrier(("sync",))
        u1 = self.op_inc("sync", lambda h, d=io["b1_ud"][0], s=U.rearrange("p a b -> p (a b)"):
                         h.dma_start(out=d, in_=s), "q10", 16)
        u2 = self.op_inc("sync", lambda h, d=io["b1_ud"][1], s=delta.rearrange("p a b -> p (a b)"):
                         h.dma_start(out=d, in_=s), "q11", 16)
        self.barrier(ENGINES)
        self.release()
        hA = self.carve_high(BF16, (NH, W))
        self.emit_b1_ffn(io, XLN, h_d, hA)
        self.release()
        if stage == "ffn":
            return

        self.emit_b1_down(io, h_d, out_d, hA)
        self.release_high()
        if stage == "down":
            return
        U = self.carve(BF16, (8, W))
        delta = self.carve(BF16, (8, W))
        self.barrier(("sync",))
        r1 = self.op_inc("sync", lambda h, d=U.rearrange("p a b -> p (a b)"), s=io["b1_ud"][0]:
                         h.dma_start(out=d, in_=s), "q10", 16)
        r2 = self.op_inc("sync", lambda h, d=delta.rearrange("p a b -> p (a b)"), s=io["b1_ud"][1]:
                         h.dma_start(out=d, in_=s), "q11", 16)
        self.w("scalar", "q10", r1)
        self.w("scalar", "q11", r2)

        dmask = self.carve(BF16, (8, W))
        t_ch = self.carve(BF16, (8, W))
        self.barrier(("scalar", "vector"))
        av = self.op("scalar", lambda h, d=t_ch, s=delta: h.activation(
            fl(d), fl(s), AF.Abs))
        self.w("vector", "ss", av)
        self.op("vector", lambda h, d=t_ch: h.tensor_scalar(
            fl(d), fl(d), 2.5, None, op0=ALU.is_le))
        self.chain("vector", lambda h, d=dmask, s0=delta, s1=t_ch: h.tensor_tensor(
            fl(d), fl(s0), fl(s1), op=ALU.mult))

        self.mark()
        UA = self.carve(BF16, (24, W))
        self.barrier(("vector",))
        self.op("vector", lambda h, d=UA[:, 0:8], s=U: h.tensor_copy(fl(d), fl(s)))
        self.op("vector", lambda h, d=UA[:, 8:16], s0=U, s1=delta: h.tensor_tensor(
            fl(d), fl(s0), fl(s1), op=ALU.mult))
        self.chain("vector", lambda h, d=t_ch, s=delta: h.scalar_tensor_tensor(
            fl(d), fl(s), 0.5, fl(delta), op0=ALU.mult, op1=ALU.mult))
        self.chain("vector", lambda h, d=UA[:, 16:24], s0=U, s1=t_ch: h.tensor_tensor(
            fl(d), fl(s0), fl(s1), op=ALU.mult))
        self.emit_apx(io["b1_apxA"], 24, lambda k, o, wd: UA[:, k, o:o + wd], out_d)
        self.release()
        if stage == "apxA":
            return

        self.mark()
        UB = self.carve(BF16, (32, W))
        self.barrier(("vector",))
        for i, (scal, srcm) in enumerate(((1 / 3, delta), (1 / 4, delta), (1 / 5, dmask), (1 / 6, delta))):
            self.chain("vector", lambda h, d=t_ch, sc=scal, m=srcm: h.scalar_tensor_tensor(
                fl(d), fl(d), sc, fl(m), op0=ALU.mult, op1=ALU.mult))
            self.chain("vector", lambda h, d=UB[:, i * 8:(i + 1) * 8], s0=U, s1=t_ch: h.tensor_tensor(
                fl(d), fl(s0), fl(s1), op=ALU.mult))
        self.emit_apx(io["b1_apxB"], 32, lambda k, o, wd: UB[:, k, o:o + wd], out_d)
        self.release()
        self.release()
        if stage == "apxB":
            return

        otile = lambda kt: out_d[kt]
        self.emit_fm_stats(otile, KD, W, False)
        if stage == "post0":
            return
        self.finalize_stats(W)
        if stage == "post1":
            return
        self.emit_resid(xtile, otile, KD, W,
                        lambda kt: xout[kt * 128:(kt + 1) * 128, :], next_stats=True)

    def dump_sbuf(self, ap, dram, shape):
        self.barrier(("sync",))
        if len(shape) == 2:
            for i in range(shape[0]):
                self.op_inc("sync", lambda h, d=dram[i], s=ap[:, i, :]: h.dma_start(
                    out=d, in_=s), f"q{i % 2}", 16)
        self.barrier(("sync",))

    def emit_b1_ffn(self, io, XLN, h_d, hA):
        W = self.W
        chs = chunks(W)
        self.mark()
        gbuf = self.carve(BF16, (2, KD, 128))
        ubuf = self.carve(BF16, (2, KD, 128))
        hs = self.carve(BF16, (2, W))
        ht = self.carve(BF16, (2, W))
        self.barrier(("gpsimd", "tensor", "scalar", "vector", "sync"))
        gW = io["b1_gw"]
        uW = io["b1_uw"]
        sdone = {}
        vdone = {}
        stv = {}
        for og in range(78):
            slot = og % 2
            if og >= 2:
                self.w("gpsimd", "ss", sdone[og - 2])
                self.w("gpsimd", "sv", vdone[og - 2])
            g1 = self.op_inc("gpsimd", lambda h, d=gbuf[:, slot], og=og: h.dma_start(
                out=d, in_=gW[og]), f"q{12 + slot}", 16)
            g2 = self.op_inc("gpsimd", lambda h, d=ubuf[:, slot], og=og: h.dma_start(
                out=d, in_=uW[og]), f"q{14 + slot}", 16)
            self.w("tensor", f"q{12 + slot}", g1)
            if og >= 1:
                self.w("tensor", "ss", sdone[og - 1])
            pgv = None
            for ci, (o, wd) in enumerate(chs):
                for k in range(KD):
                    fn = lambda h, p=self.PS[:, ci, 0:wd], l=gbuf[:, slot, k, :], \
                        r=XLN[:, k, o:o + wd], st=(k == 0), sp_=(k == KD - 1): \
                        h.matmul(p, l, r, start=st, stop=sp_)
                    if ci == len(chs) - 1 and k == KD - 1:
                        pgv = self.op_inc("tensor", fn, "sp", 1)
                    else:
                        self.op("tensor", fn)
            self.w("tensor", f"q{14 + slot}", g2)
            if og >= 1:
                self.w("tensor", "sv", vdone[og - 1])
            puv = None
            for ci, (o, wd) in enumerate(chs):
                for k in range(KD):
                    fn = lambda h, p=self.PS[:, 3 + ci, 0:wd], l=ubuf[:, slot, k, :], \
                        r=XLN[:, k, o:o + wd], st=(k == 0), sp_=(k == KD - 1): \
                        h.matmul(p, l, r, start=st, stop=sp_)
                    if ci == len(chs) - 1 and k == KD - 1:
                        puv = self.op_inc("tensor", fn, "sp", 1)
                    else:
                        self.op("tensor", fn)
            self.w("scalar", "sp", pgv)
            if og >= 2:
                self.w("scalar", "sv", vdone[og - 2])
            sv = None
            for ci, (o, wd) in enumerate(chs):
                sv = self.op_inc("scalar", lambda h, d=hs[:, slot, o:o + wd],
                                 s=self.PS[:, ci, 0:wd]: h.activation(d, s, AF.Silu), "ss", 1)
            sdone[og] = sv
            self.w("vector", "sp", puv)
            self.w("vector", "ss", sv)
            if og >= 2:
                self.w("vector", f"q{4 + slot}", stv[og - 2])
            vv = None
            hdst = hA[:, og, :] if og < 22 else ht[:, slot, :]
            for ci, (o, wd) in enumerate(chs):
                vv = self.op_inc("vector", lambda h, d=hdst[:, o:o + wd],
                                 s0=hs[:, slot, o:o + wd], s1=self.PS[:, 3 + ci, 0:wd]:
                                 h.tensor_tensor(d, s0, s1, op=ALU.mult), "sv", 1)
            vdone[og] = vv
            if og >= 22:
                self.w("sync", "sv", vv)
                stv[og] = self.op_inc("sync", lambda h, d=h_d[og], s=ht[:, slot]: h.dma_start(
                    out=d, in_=s), f"q{4 + slot}", 16)
            else:
                stv[og] = None
        self.release()

    SERIAL_DOWN = False

    def emit_b1_down(self, io, h_d, out_d, hA):
        W = self.W
        chs = chunks(W)
        self.mark()
        dbuf = self.carve(BF16, (2, 78, 256))
        hring = self.carve(BF16, (8, W))
        self.barrier(("gpsimd", "tensor", "scalar", "sync"))
        dW = io["b1_dw"]
        edone = {}
        kdone = {}
        stv = {}
        for og in range(16):
            slot = og % 2
            if self.SERIAL_DOWN:
                self.barrier(ENGINES)
            if og >= 2:
                self.w("gpsimd", "ss", edone[og - 2])
            gv = self.op_inc("gpsimd", lambda h, d=dbuf[:, slot], og=og: h.dma_start(
                out=d, in_=dW[og]), f"q{12 + slot}", 16)
            NH = 22
            NS = 78 - NH
            for k in range(78):
                if k >= NH:
                    ks = k - NH
                    hslot = ks % 8
                    lastcong = (NS - 1) - ((NS - 1 - hslot) % 8)
                    prev = (og, NH + ks - 8) if ks >= 8 else (
                        (og - 1, NH + lastcong) if og >= 1 else None)
                    if prev in kdone:
                        self.w("sync", "sp", kdone[prev])
                    hv = self.op_inc("sync", lambda h, d=hring[:, hslot], k=k: h.dma_start(
                        out=d, in_=h_d[k]), f"q{hslot}", 16)
                    self.w("tensor", f"q{hslot}", hv)
                    rtile = hring[:, hslot]
                else:
                    rtile = hA[:, k]
                if k == 0:
                    self.w("tensor", f"q{12 + slot}", gv)
                    if og >= 1:
                        self.w("tensor", "ss", edone[og - 1])
                for ot in range(2):
                    for ci, (o, wd) in enumerate(chs):
                        fn = lambda h, p=self.PS[:, ot * 3 + ci, 0:wd], \
                            l=dbuf[:, slot, k, ot * 128:(ot + 1) * 128], \
                            r=rtile[:, o:o + wd], st=(k == 0), sp_=(k == 77): \
                            h.matmul(p, l, r, start=st, stop=sp_)
                        if ot == 1 and ci == len(chs) - 1:
                            kdone[(og, k)] = self.op_inc("tensor", fn, "sp", 1)
                        else:
                            self.op("tensor", fn)
            pv = kdone[(og, 77)]
            self.w("scalar", "sp", pv)
            ev = None
            for ot in range(2):
                if og >= 1:
                    self.w("scalar", f"q{8 + ot}", stv.get((og - 1, ot)))
                psflat = self.PS[:, ot * 3, 0:512].rearrange("p a -> p a") if False else None
                src_ap = self.PS[:, ot * 3:(ot + 1) * 3, :].rearrange("p a b -> p (a b)")[:, 0:W]
                ev = self.op_inc("scalar", lambda h, d=self.ring_a[:, ot, 0:W],
                                 s=src_ap: h.copy(d, s), "ss", 1)
                self.w("scalar", "ss", ev)
                stv[(og, ot)] = self.op_inc("scalar", lambda h, d=out_d[og * 2 + ot],
                                            s=self.ring_a[:, ot]: h.dma_start(out=d, in_=s), f"q{8 + ot}", 16)
            edone[og] = ev
        self.release()

    def emit_apx(self, w_dram, KT, rhs_fn, out_d):
        W = self.W
        chs = chunks(W)
        self.mark()
        wbuf = self.carve(BF16, (2, KT, 128))
        self.barrier(("gpsimd", "tensor", "vector", "sync", "scalar"))
        vdone = {}
        stv = {}
        ldv = {}
        for og in range(32):
            slot = og % 2
            bank0 = (og % 2) * 3
            if og >= 2:
                self.w("gpsimd", "sv", vdone[og - 2])
            gv = self.op_inc("gpsimd", lambda h, d=wbuf[:, slot], og=og: h.dma_start(
                out=d, in_=w_dram[og]), f"q{12 + slot}", 16)
            if og >= 2:
                self.w("sync", "sv", vdone[og - 2])
            ldv[og] = self.op_inc("sync", lambda h, d=self.ring_b[:, slot], og=og: h.dma_start(
                out=d, in_=out_d[og]), f"q{2 + slot}", 16)
            self.w("tensor", f"q{12 + slot}", gv)
            if og >= 2:
                self.w("tensor", "sv", vdone[og - 2])
            pv = None
            for ci, (o, wd) in enumerate(chs):
                for k in range(KT):
                    fn = lambda h, p=self.PS[:, bank0 + ci, 0:wd], l=wbuf[:, slot, k, :], \
                        r=rhs_fn(k, o, wd), st=(k == 0), sp_=(k == KT - 1): \
                        h.matmul(p, l, r, start=st, stop=sp_)
                    if ci == len(chs) - 1 and k == KT - 1:
                        pv = self.op_inc("tensor", fn, "sp", 1)
                    else:
                        self.op("tensor", fn)
            self.w("vector", "sp", pv)
            self.w("vector", f"q{2 + slot}", ldv[og])
            if og >= 2:
                self.w("vector", f"q{8 + slot}", stv[og - 2])
            vv = None
            for ci, (o, wd) in enumerate(chs):
                vv = self.op_inc("vector", lambda h, d=self.ring_a[:, slot, o:o + wd],
                                 s0=self.ring_b[:, slot, o:o + wd],
                                 s1=self.PS[:, bank0 + ci, 0:wd]:
                                 h.tensor_tensor(d, s1, s0, op=ALU.add), "sv", 1)
            vdone[og] = vv
            self.w("scalar", "sv", vv)
            stv[og] = self.op_inc("scalar", lambda h, d=out_d[og], s=self.ring_a[:, slot]:
                                  h.dma_start(out=d, in_=s), f"q{8 + slot}", 16)
        self.release()


def prep_b1(inp):
    up = np.ascontiguousarray(inp["L0_attn_up_w"].T)
    gate = np.ascontiguousarray(inp["L0_attn_gate_w"].T)
    lao = np.ascontiguousarray(inp["L0_attn_local_approx_output"].T)
    fuse = inp["L0_attn_fuse_weight"]
    fT = [np.ascontiguousarray(fuse[:, :, i].T) for i in range(6)]
    apxA = np.concatenate([lao, fT[0], fT[1]], axis=0)
    apxB = np.concatenate([fT[2], fT[3], fT[4], fT[5]], axis=0)
    return {
        "b1_up": tile_lhsT(up, 128),
        "b1_gate": tile_lhsT(gate, 128),
        "b1_lp": np.ascontiguousarray(inp["L0_attn_local_point"].reshape(8, 128).T),
        "b1_gw": tile_lhsT(np.ascontiguousarray(inp["L0_attn_ffn_gate_w"].T), 128),
        "b1_uw": tile_lhsT(np.ascontiguousarray(inp["L0_attn_ffn_up_w"].T), 128),
        "b1_dw": tile_lhsT(np.ascontiguousarray(inp["L0_attn_ffn_down_w"].T), 256),
        "b1_apxA": tile_lhsT(apxA, 128),
        "b1_apxB": tile_lhsT(apxB, 128),
    }


def core_x_slices(x):
    xf = np.ascontiguousarray(x.reshape(8192, 4096).T)
    outs = []
    for c in range(8):
        s = c // 2
        xs = xf[:, 2048 * s:2048 * (s + 1)]
        pad = np.zeros((4096, HALO), np.float32)
        padded = np.concatenate([pad, xs, pad], axis=1)
        if c % 2 == 0:
            outs.append(np.ascontiguousarray(padded[:, 0:TW]))
        else:
            outs.append(np.ascontiguousarray(padded[:, OWN:OWN + TW]))
    return outs


def build_b1(width=TW, stage="full"):
    nc = bass.Bass("TRN2", target_bir_lowering=False)
    io = {}
    io["xin"] = nc.dram_tensor("xin", [4096, width], F32, kind="ExternalInput").ap()
    io["xout"] = nc.dram_tensor("xout", [4096, width], F32, kind="ExternalOutput").ap()
    okind = "ExternalOutput" if stage in ("ffn", "down", "apxA", "apxB") else "Internal"
    io["b1_out"] = nc.dram_tensor("b1_out", [32, 128, width], BF16, kind=okind).ap()
    io["b1_ud"] = nc.dram_tensor("b1_ud", [2, 128, 8 * width], BF16, kind="Internal").ap()
    io["b1_h"] = nc.dram_tensor("b1_h", [78, 128, width], BF16,
                                kind="ExternalOutput" if stage == "ffn" else "Internal").ap()
    io["dbg"] = nc.dram_tensor("dbg", [KD, 128, width], BF16, kind="ExternalOutput").ap()
    io["dbg2"] = nc.dram_tensor("dbg2", [KD, 128, width], BF16, kind="ExternalOutput").ap()
    for nm, shape in (("b1_up", [8, 128, KD * 128]), ("b1_gate", [8, 128, KD * 128]),
                      ("b1_gw", [78, 128, KD * 128]), ("b1_uw", [78, 128, KD * 128]),
                      ("b1_dw", [16, 128, 78 * 256]), ("b1_apxA", [32, 128, 24 * 128]),
                      ("b1_apxB", [32, 128, 32 * 128]), ("b1_lp", [128, 8])):
        io[nm] = nc.dram_tensor(nm, shape, F32, kind="ExternalInput").ap()
    b = B(nc, width)
    b.emit_init()
    b.emit_block1(io, stage=stage)
    b.barrier(("sync",))
    with nc.Block() as block:
        b.build(block)
    return nc



def emit_block2(self, io, stage="full", pre_stats_done=False):
    W = self.W
    chs = chunks(W)
    xfm = io["xin"]
    xout = io["xout"]
    out_d = io["b_out"]
    NT = (W + 127) // 128

    self.mark()
    self.barrier(("gpsimd",))
    XLN = self.carve(BF16, (KD, W))
    wrb = self.carve(BF16, (KD, 64))
    rb = self.carve(F32, (1,))
    lgt = self.carve(F32, (NT, 64))
    sm = self.carve(F32, (NT, 64))
    wk = self.carve(F32, (64,))
    mxc = self.carve(F32, (16,))
    kp = self.carve(F32, (NT, 72))
    ktb = self.carve(BF16, (W,))
    wsr = self.carve(F32, (W,))
    ebb = self.carve(BF16, (4096,))
    nmrwb = self.carve(BF16, (W,))
    RWB = self.carve(F32, (W,))

    self.op_inc("gpsimd", lambda h, d=wrb.rearrange("p a b -> p (a b)"),
                s=io["b2_rw"]: h.dma_start(out=d, in_=s), "q12", 16)
    self.op_inc("gpsimd", lambda h, d=rb[0:64], s=io["b2_rb"]: h.dma_start(
        out=d, in_=s), "q13", 16)
    self.op_inc("gpsimd", lambda h, d=ebb[0:64], s=io["b2_eb"]: h.dma_start(
        out=d, in_=s), "q14", 16)

    def xtile(kt):
        return xfm[kt * 128:(kt + 1) * 128, :]

    if not pre_stats_done:
        self.emit_fm_stats(xtile, KD, W, True)
    self.finalize_stats(W)
    self.emit_ln_apply(xtile, KD, W, lambda kt: XLN[:, kt, :])

    sn = self.barrier(("tensor", "vector"))
    pv = None
    for ci, (o, wd) in enumerate(chs):
        for k in range(KD):
            fn = lambda h, p=self.PS[0:64, ci, 0:wd], l=wrb[:, k, :], \
                r=XLN[:, k, o:o + wd], st=(k == 0), sp_=(k == KD - 1): \
                h.matmul(p, l, r, start=st, stop=sp_)
            if ci == len(chs) - 1 and k == KD - 1:
                pv = self.op_inc("tensor", fn, "sp", 1)
            else:
                self.op("tensor", fn)
    self.w("vector", "sp", pv)
    lg = self.carve(F32, (W,))
    for ci, (o, wd) in enumerate(chs):
        self.op("vector", lambda h, d=lg[0:64, o:o + wd], s=self.PS[0:64, ci, 0:wd]:
                h.tensor_scalar(d, s, rb[0:64, 0:1], None, op0=ALU.add))
    vv = self.sems["sv"].n

    self.w("tensor", "sv", vv)
    evc = {}
    for tt in range(NT):
        tw = min(128, W - tt * 128)
        bk = (3, 6, 7)[tt % 3]
        if tt >= 3:
            self.w("tensor", "sv", evc[tt - 3])
        tv = self.op_inc("tensor", lambda h, p=self.PS[0:tw, bk, 0:64],
                         s=lg[0:64, tt * 128:tt * 128 + tw], i=self.ident32[0:64, 0:64]:
                         h.transpose(p, s, i), "sp", 1)
        self.w("vector", "sp", tv)
        evc[tt] = self.op("vector", lambda h, d=lgt[0:tw, tt, :], s=self.PS[0:tw, bk, 0:64]:
                          h.tensor_copy(d, s))
    mxs = {}
    for tt in range(NT):
        tw = min(128, W - tt * 128)
        self.w("vector", "sv", evc[tt])
        mxs[tt] = self.op("vector", lambda h, d=mxc[0:tw, tt:tt + 1], s=lgt[0:tw, tt, :]:
                          h.reduce_max(d, s, axis=AX.X, negate=True))
    for tt in range(NT):
        tw = min(128, W - tt * 128)
        self.w("scalar", "sv", mxs[tt])
        self.op("scalar", lambda h, d=sm[0:tw, tt, :], s=lgt[0:tw, tt, :],
                b=mxc[0:tw, tt:tt + 1]: h.activation(d, s, AF.Exp, bias=b))
    sums = {}
    for tt in range(NT):
        tw = min(128, W - tt * 128)
        self.w("vector", "ss", self.sems["ss"].n - NT + tt + 1)
        sums[tt] = self.op("vector", lambda h, d=wk[0:tw, 32 + tt:33 + tt], s=sm[0:tw, tt, :]:
                           h.reduce_sum(d, s, axis=AX.X))
    rec = {}
    for tt in range(NT):
        tw = min(128, W - tt * 128)
        self.w("vector", "sv", sums[tt])
        rec[tt] = self.op("vector", lambda h, d=wk[0:tw, 48 + tt:49 + tt], s=wk[0:tw, 32 + tt:33 + tt]:
                          h.reciprocal(d, s))
    prb = {}
    for tt in range(NT):
        tw = min(128, W - tt * 128)
        self.w("vector", "sv", rec[tt])
        prb[tt] = self.op("vector", lambda h, d=sm[0:tw, tt, :], s=sm[0:tw, tt, :],
                          r=wk[0:tw, 48 + tt:49 + tt]: h.tensor_scalar(d, s, r, None, op0=ALU.mult))
    cps = {}
    for tt in range(NT):
        tw = min(128, W - tt * 128)
        self.w("vector", "sv", prb[tt])
        cps[tt] = self.op("vector", lambda h, d=lgt[0:tw, tt, :], s=sm[0:tw, tt, :]:
                          h.tensor_copy(d, s))
    lastv = dict(cps)
    for j in range(8):
        mx2 = {}
        for tt in range(NT):
            tw = min(128, W - tt * 128)
            self.w("vector", "sv", lastv[tt])
            mx2[tt] = self.op("vector", lambda h, d=kp[0:tw, tt, 70:71], s=lgt[0:tw, tt, :]:
                              h.reduce_max(d, s, axis=AX.X))
        if j < 7:
            msk = {}
            for tt in range(NT):
                tw = min(128, W - tt * 128)
                self.w("vector", "sv", mx2[tt])
                msk[tt] = self.op("vector", lambda h, d=kp[0:tw, tt, 0:64], s=lgt[0:tw, tt, :],
                                  m=kp[0:tw, tt, 70:71]: h.tensor_scalar(d, s, m, None, op0=ALU.is_ge))
            for tt in range(NT):
                tw = min(128, W - tt * 128)
                self.w("vector", "sv", msk[tt])
                lastv[tt] = self.op("vector", lambda h, d=lgt[0:tw, tt, :], s=kp[0:tw, tt, 0:64],
                                    w2=lgt[0:tw, tt, :]: h.scalar_tensor_tensor(
                    d, s, -4.0, w2, op0=ALU.mult, op1=ALU.add))
        else:
            lastv = mx2
    kps = {}
    for tt in range(NT):
        tw = min(128, W - tt * 128)
        self.w("vector", "sv", lastv[tt])
        kps[tt] = self.op("vector", lambda h, d=lgt[0:tw, tt, :], s=sm[0:tw, tt, :],
                          m=kp[0:tw, tt, 70:71]: h.tensor_scalar(d, s, m, None, op0=ALU.is_ge))
    km = {}
    for tt in range(NT):
        tw = min(128, W - tt * 128)
        self.w("vector", "sv", kps[tt])
        km[tt] = self.op("vector", lambda h, d=kp[0:tw, tt, 0:64], s0=lgt[0:tw, tt, :],
                         s1=sm[0:tw, tt, :]: h.tensor_tensor(d, s0, s1, op=ALU.mult))
    ws = {}
    for tt in range(NT):
        tw = min(128, W - tt * 128)
        self.w("vector", "sv", km[tt])
        ws[tt] = self.op("vector", lambda h, d=kp[0:tw, tt, 64:65], s=kp[0:tw, tt, 0:64]:
                         h.reduce_sum(d, s, axis=AX.X))
    bkev = {}
    for tt in range(NT):
        tw = min(128, W - tt * 128)
        bk = (3, 6, 7)[tt % 3]
        self.w("tensor", "sv", ws[tt])
        if tt >= 3:
            self.w("tensor", "ss", bkev[tt - 3])
        tv2 = self.op_inc("tensor", lambda h, p=self.PS[0:65, bk, 0:tw],
                          s=kp[0:tw, tt, 0:65], i=self.ident32[0:tw, 0:tw]:
                          h.transpose(p, s, i), "sp", 1)
        self.w("scalar", "sp", tv2)
        self.op("scalar", lambda h, d=ktb[0:64, tt * 128:tt * 128 + tw],
                s=self.PS[0:64, bk, 0:tw]: h.copy(d, s))
        bkev[tt] = self.op("scalar", lambda h, d=wsr[0:1, tt * 128:tt * 128 + tw],
                           s=self.PS[64:65, bk, 0:tw]: h.copy(d, s))

    self.barrier(("vector", "scalar", "tensor"))
    m_row = self.srow[:, 0, 0:W]
    r_row = self.srow[:, 1, 0:W]
    rw_row = self.srow[:, 2, 0:W]
    t_row = self.srow[:, 3, 0:W]
    self.op("vector", lambda h, d=rw_row, s0=r_row, s1=wsr[0:1, 0:W]:
            h.tensor_tensor(d, s0, s1, op=ALU.mult))
    self.chain("vector", lambda h, d=t_row, s0=m_row, s1=rw_row:
               h.tensor_tensor(d, s0, s1, op=ALU.mult))
    self.chain("vector", lambda h, d=nmrwb[0:1, 0:W], s=t_row: h.tensor_copy(d, s))
    bv = self.chain("vector", lambda h, d=t_row, s=rw_row: h.tensor_copy(d, s))
    self.w("tensor", "sv", bv)
    pv2 = None
    for ci, (o, wd) in enumerate(chs):
        fn = lambda h, p=self.PS[:, ci, 0:wd], r=rw_row[:, o:o + wd]: \
            h.matmul(p, self.ones_row32, r, start=True, stop=True)
        if ci == len(chs) - 1:
            pv2 = self.op_inc("tensor", fn, "sp", 1)
        else:
            self.op("tensor", fn)
    self.w("vector", "sp", pv2)
    for ci, (o, wd) in enumerate(chs):
        self.op("vector", lambda h, d=RWB[:, o:o + wd], s=self.PS[:, ci, 0:wd]:
                h.tensor_copy(d, s))

    self.barrier(("tensor", "vector", "sync", "scalar"))
    vdone = {}
    stv = {}
    for og in range(32):
        par = og % 2
        bank0 = par * 3
        slot = og % 2
        if og >= 2:
            self.w("tensor", "sv", vdone[og - 2])
        pv3 = None
        for ci, (o, wd) in enumerate(chs):
            self.op("tensor", lambda h, p=self.PS[:, bank0 + ci, 0:wd],
                    l=ebb[0:64, og * 128:(og + 1) * 128], r=ktb[0:64, o:o + wd]:
                    h.matmul(p, l, r, start=True, stop=False))
            pv3 = self.op_inc("tensor", lambda h, p=self.PS[:, bank0 + ci, 0:wd],
                              l=self.ones_row16[0:1, 0:128], r=nmrwb[0:1, o:o + wd]:
                              h.matmul(p, l, r, start=False, stop=True), "sp", 1)
        if og >= 2:
            self.w("sync", "sv", vdone[og - 2])
        xv = self.op_inc("sync", lambda h, d=self.ring_x[:, slot, 0:W], og=og:
                         h.dma_start(out=d, in_=xfm[og * 128:(og + 1) * 128, :]), f"q{slot}", 16)
        self.w("vector", f"q{slot}", xv)
        self.w("vector", "sp", pv3)
        if og >= 2:
            self.w("vector", f"q{8 + slot}", stv[og - 2])
        self.op("vector", lambda h, d=self.ring_c[:, slot, 0:W], s0=self.ring_x[:, slot, 0:W],
                s1=RWB[:, 0:W]: h.tensor_tensor(d, s0, s1, op=ALU.mult))
        vv3 = None
        for ci, (o, wd) in enumerate(chs):
            vv3 = self.chain("vector", lambda h, d=self.ring_a[:, slot, o:o + wd],
                             s0=self.ring_c[:, slot, o:o + wd], s1=self.PS[:, bank0 + ci, 0:wd]:
                             h.tensor_tensor(d, s0, s1, op=ALU.add))
        vdone[og] = vv3
        self.w("scalar", "sv", vv3)
        stv[og] = self.op_inc("scalar", lambda h, d=out_d[og], s=self.ring_a[:, slot, 0:W]:
                              h.dma_start(out=d, in_=s), f"q{8 + slot}", 16)
    self.release()

    otile = lambda kt: out_d[kt]
    self.emit_fm_stats(otile, KD, W, False)
    self.finalize_stats(W)
    self.emit_resid(xtile, otile, KD, W,
                    lambda kt: xout[kt * 128:(kt + 1) * 128, :], next_stats=True)


def prep_b2(inp):
    rw = np.ascontiguousarray(inp["L0_ffn_router_weight"].T)
    rwt = rw.reshape(KD, 128, 64).transpose(1, 0, 2).reshape(128, KD * 64)
    return {
        "b2_rw": np.ascontiguousarray(rwt),
        "b2_rb": np.ascontiguousarray(inp["L0_ffn_router_bias"].reshape(64, 1)),
        "b2_eb": np.ascontiguousarray(inp["L0_ffn_expert_bias"]),
    }


B.emit_block2 = emit_block2


def build_b2(width=TW, stage="full"):
    nc = bass.Bass("TRN2", target_bir_lowering=False)
    io = {}
    io["xin"] = nc.dram_tensor("xin", [4096, width], F32, kind="ExternalInput").ap()
    io["xout"] = nc.dram_tensor("xout", [4096, width], F32, kind="ExternalOutput").ap()
    io["b_out"] = nc.dram_tensor("b_out", [32, 128, width], BF16, kind="Internal").ap()
    io["b2_rw"] = nc.dram_tensor("b2_rw", [128, KD * 64], F32, kind="ExternalInput").ap()
    io["b2_rb"] = nc.dram_tensor("b2_rb", [64, 1], F32, kind="ExternalInput").ap()
    io["b2_eb"] = nc.dram_tensor("b2_eb", [64, 4096], F32, kind="ExternalInput").ap()
    b = B(nc, width)
    b.emit_init()
    b.emit_block2(io, stage=stage)
    b.barrier(("sync",))
    with nc.Block() as block:
        b.build(block)
    return nc


def emit_block3(self, io, stage="full", pre_stats_done=False):
    W = self.W
    WO = OWN
    cho = chunks(WO)
    xfm = io["xin"]
    xout = io["xout"]
    out_d = io["b_out"]

    self.mark()
    self.barrier(("gpsimd",))
    qc = self.carve_high(BF16, (8, WO))
    kc = self.carve_high(BF16, (8, WO))
    vc = self.carve_high(BF16, (8, WO))
    cb = self.carve(F32, (24,))
    bqf = self.carve(F32, (8,))
    bkf = self.carve(F32, (8,))
    bof = self.carve(F32, (8,))
    pbf = self.carve(F32, (32,))
    edge = self.carve(F32, (2,))
    self.op_inc("gpsimd", lambda h, d=cb, s=io["b3_cb"]: h.dma_start(out=d, in_=s), "q14", 16)
    self.op_inc("gpsimd", lambda h, d=bqf, s=io["b3_bq"]: h.dma_start(out=d, in_=s), "q14", 16)
    self.op_inc("gpsimd", lambda h, d=bkf, s=io["b3_bk"]: h.dma_start(out=d, in_=s), "q14", 16)
    self.op_inc("gpsimd", lambda h, d=bof, s=io["b3_bo"]: h.dma_start(out=d, in_=s), "q14", 16)
    self.op_inc("gpsimd", lambda h, d=pbf, s=io["b3_pb"]: h.dma_start(out=d, in_=s), "q14", 16)
    self.op_inc("gpsimd", lambda h, d=edge, s=io["b3_edge"]: h.dma_start(out=d, in_=s), "q14", 16)
    bvrow = self.carve(BF16, (1024,))
    self.op_inc("gpsimd", lambda h, d=bvrow[0:1], s=io["b3_bv"]: h.dma_start(out=d, in_=s), "q15", 16)

    def xtile(kt):
        return xfm[kt * 128:(kt + 1) * 128, :]

    self.mark()
    XLN = self.carve(BF16, (KD, W))
    if not pre_stats_done:
        self.emit_fm_stats(xtile, KD, W, True)
    self.finalize_stats(W)
    self.emit_ln_apply(xtile, KD, W, lambda kt: XLN[:, kt, :])
    self.barrier(("vector",))
    for k in range(KD):
        self.op("vector", lambda h, d=XLN[:, k, 15:16], m=edge[:, 0:1]:
                h.tensor_scalar(d, d, m, None, op0=ALU.mult))
        self.op("vector", lambda h, d=XLN[:, k, 1040:1041], m=edge[:, 1:2]:
                h.tensor_scalar(d, d, m, None, op0=ALU.mult))
    if stage == "ln":
        return
    self.mark()
    cwb = self.carve(BF16, (4, KD, 128))
    self.barrier(("gpsimd", "tensor", "scalar"))
    cW = io["b3_cw"]
    edone = {}
    jdone = {}
    for og in range(24):
        bank0 = (og % 2) * 2
        for j in range(3):
            gi = og * 3 + j
            slot = gi % 4
            prev = None if gi < 4 else ((gi - 4) // 3, (gi - 4) % 3)
            if prev in jdone:
                self.w("gpsimd", "sp", jdone[prev])
            gv = self.op_inc("gpsimd", lambda h, d=cwb[:, slot], og=og, j=j: h.dma_start(
                out=d, in_=cW[og, j]), f"q{12 + slot}", 16)
            self.w("tensor", f"q{12 + slot}", gv)
            if j == 0 and og >= 2:
                self.w("tensor", "ss", edone[og - 2])
            for k in range(KD):
                for ci, (o, wd) in enumerate(cho):
                    fn = lambda h, p=self.PS[:, bank0 + ci, 0:wd], \
                        l=cwb[:, slot, k, :], \
                        r=XLN[:, k, 15 + j + o:15 + j + o + wd], \
                        st=(j == 0 and k == 0), sp_=(j == 2 and k == KD - 1): \
                        h.matmul(p, l, r, start=st, stop=sp_)
                    if k == KD - 1 and ci == 1:
                        jdone[(og, j)] = self.op_inc("tensor", fn, "sp", 1)
                    else:
                        self.op("tensor", fn)
        self.w("scalar", "sp", jdone[(og, 2)])
        gt = og
        dstt = qc if gt < 8 else (kc if gt < 16 else vc)
        lt = gt % 8
        ev = self.op("scalar", lambda h, d=dstt[:, lt, 0:WO],
                     s=self.PS[:, bank0:bank0 + 2, :].rearrange("p a b -> p (a b)")[:, 0:WO],
                     gt=gt: h.activation(d, s, AF.Identity, bias=cb[:, gt:gt + 1]))
        edone[og] = ev
    self.release()
    self.release()

    if stage == "conv":
        return
    qhb = self.carve(BF16, (8, WO))
    khb = self.carve(BF16, (8, WO))
    vhb = self.carve(BF16, (8, 1024))
    self.mark()
    wbuf = self.carve(BF16, (2, 8, 256))
    self.barrier(("gpsimd", "tensor", "scalar"))

    def proj_phase(wdram, src, dstt, bias_fm, scale):
        self.barrier(("gpsimd", "tensor", "scalar"))
        evg = {}
        for og in range(4):
            slot = og % 2
            bank0 = (og % 2) * 4
            if og >= 2:
                self.w("gpsimd", "ss", evg[og - 2])
            gv = self.op_inc("gpsimd", lambda h, d=wbuf[:, slot], og=og: h.dma_start(
                out=d, in_=wdram[og]), f"q{12 + slot}", 16)
            self.w("tensor", f"q{12 + slot}", gv)
            if og >= 2:
                self.w("tensor", "ss", evg[og - 2])
            pv = None
            for ot in range(2):
                for ci, (o, wd) in enumerate(cho):
                    for k in range(8):
                        fn = lambda h, p=self.PS[:, bank0 + ot * 2 + ci, 0:wd], \
                            l=wbuf[:, slot, k, ot * 128:(ot + 1) * 128], \
                            r=src[:, k, o:o + wd], st=(k == 0), sp_=(k == 7): \
                            h.matmul(p, l, r, start=st, stop=sp_)
                        if ot == 1 and ci == 1 and k == 7:
                            pv = self.op_inc("tensor", fn, "sp", 1)
                        else:
                            self.op("tensor", fn)
            self.w("scalar", "sp", pv)
            ev = None
            for ot in range(2):
                gt = og * 2 + ot
                for ci, (o, wd) in enumerate(cho):
                    if scale is None:
                        ev = self.op("scalar", lambda h, d=dstt[:, gt, o:o + wd],
                                     s=self.PS[:, bank0 + ot * 2 + ci, 0:wd], gt=gt:
                                     h.activation(d, s, AF.Identity, bias=bias_fm[:, gt:gt + 1]))
                    else:
                        ev = self.op("scalar", lambda h, d=dstt[:, gt, o:o + wd],
                                     s=self.PS[:, bank0 + ot * 2 + ci, 0:wd], gt=gt:
                                     h.activation(d, s, AF.Identity, bias=bias_fm[:, gt:gt + 1],
                                                  scale=scale))
            evg[og] = ev
        return evg

    self.op("vector", lambda h, d=bqf: h.tensor_scalar(d, d, 0.125, None, op0=ALU.mult))
    self.barrier(("tensor", "gpsimd", "scalar"))
    proj_phase(io["b3_wq"], qc, qhb, bqf, 0.125)
    proj_phase(io["b3_wk"], kc, khb, bkf, None)

    wvb = self.carve(BF16, (8, 1024))
    gv2 = self.op_inc("gpsimd", lambda h, d=wvb.rearrange("p a b -> p (a b)"),
                      s=io["b3_wv"]: h.dma_start(out=d, in_=s), "q14", 16)
    self.barrier(("tensor", "scalar"))
    evt = {}
    for tt in range(8):
        par = tt % 2
        bank0 = par * 2
        if tt >= 2:
            self.w("tensor", "ss", evt[tt - 2])
        pv = None
        for ci in range(2):
            for k in range(8):
                fn = lambda h, p=self.PS[:, bank0 + ci, 0:512], \
                    l=vc[:, k, tt * 128:(tt + 1) * 128], \
                    r=wvb[:, k, ci * 512:(ci + 1) * 512], st=(k == 0): \
                    h.matmul(p, l, r, start=st, stop=False)
                self.op("tensor", fn)
            fn2 = lambda h, p=self.PS[:, bank0 + ci, 0:512], l=self.ones_row16[0:1, 0:128], \
                r=bvrow[0:1, ci * 512:(ci + 1) * 512]: h.matmul(p, l, r, start=False, stop=True)
            if ci == 1:
                pv = self.op_inc("tensor", fn2, "sp", 1)
            else:
                self.op("tensor", fn2)
        self.w("scalar", "sp", pv)
        ev = None
        for ci in range(2):
            ev = self.op("scalar", lambda h, d=vhb[:, tt, ci * 512:(ci + 1) * 512],
                         s=self.PS[:, bank0 + ci, 0:512]: h.copy(d, s))
        evt[tt] = ev
    self.release()
    self.release_high()

    if stage == "qkv":
        return
    sub = 9
    if stage.startswith("attn") and len(stage) > 4:
        sub = int(stage[4])
    sm = self.carve(F32, (16, 128))
    pb = self.carve(BF16, (16, 128))
    pTs = self.carve(BF16, (16, 128))
    mxb = self.carve(F32, (20,))
    ssb = self.carve(F32, (16,))
    oT = self.carve(BF16, (8, 1024))
    PSB = self.PS.bitcast(BF16)
    self.barrier(("tensor", "vector", "scalar"))
    maskdone = {}
    ptevdone = {}
    otdone = {}
    for b in range(8):
        if b >= 1:
            self.w("tensor", "sv", maskdone[b - 1])
        sv0 = None
        for hh in range(16):
            fn = lambda h, p=self.PS[:, hh % 4, (hh // 4) * 128:(hh // 4) * 128 + 128], \
                l=qhb[64 * (hh % 2):64 * (hh % 2) + 64, hh // 2, b * 128:(b + 1) * 128], \
                r=khb[64 * (hh % 2):64 * (hh % 2) + 64, hh // 2, b * 128:(b + 1) * 128]: \
                h.matmul(p, l, r, start=True, stop=True)
            if hh == 15:
                sv0 = self.op_inc("tensor", fn, "sp", 1)
            else:
                self.op("tensor", fn)
        self.w("vector", "sp", sv0)
        for hh in range(16):
            mv = self.op("vector", lambda h, d=sm[:, hh, :], s0=self.PS[:, hh % 4, (hh // 4) * 128:(hh // 4) * 128 + 128],
                         s1=self.causal: h.tensor_tensor(d, s0, s1, op=ALU.add))
        maskdone[b] = mv
        self.chain("vector", lambda h, d=mxb[:, 0:16], s=sm.rearrange("p a b -> p a b"):
                   h.reduce_max(d.rearrange("p (a o) -> p a o", o=1), s, axis=AX.X, negate=True))
        mv = self.sems["sv"].n
        self.w("scalar", "sv", mv)
        ev2 = None
        for hh in range(16):
            ev2 = self.op("scalar", lambda h, d=pb[:, hh, :], s=sm[:, hh, :],
                          bi=mxb[:, hh:hh + 1]: h.activation(d, s, AF.Exp, bias=bi))
        self.w("vector", "ss", ev2)
        self.chain("vector", lambda h, d=ssb[:, 0:16], s=pb:
                   h.reduce_sum(d.rearrange("p (a o) -> p a o", o=1), s, axis=AX.X))
        self.chain("vector", lambda h, d=mxb[:, 16:17] if False else ssb[:, 0:16], s=ssb[:, 0:16]:
                   h.reciprocal(d, s))
        for hh in range(16):
            self.chain("vector", lambda h, d=pb[:, hh, :], s=pb[:, hh, :],
                       r=ssb[:, hh:hh + 1]: h.tensor_scalar(d, s, r, None, op0=ALU.mult)) if hh == 0 else \
                self.op("vector", lambda h, d=pb[:, hh, :], s=pb[:, hh, :],
                        r=ssb[:, hh:hh + 1]: h.tensor_scalar(d, s, r, None, op0=ALU.mult))
        pnv = self.sems["sv"].n
        self.w("tensor", "sv", pnv)
        if b >= 1:
            self.w("tensor", "ss", ptevdone[b - 1])
        tv = None
        for hh in range(16):
            fn = lambda h, p=PSB[:, 4 + hh // 8, (hh % 8) * 128:(hh % 8) * 128 + 128], \
                s=pb[:, hh, :], i=self.ident16: h.transpose(p, s, i)
            if hh == 15:
                tv = self.op_inc("tensor", fn, "sp", 1)
            else:
                self.op("tensor", fn)
        self.w("scalar", "sp", tv)
        pev = None
        for hh in range(16):
            pev = self.op("scalar", lambda h, d=pTs[:, hh, :],
                          s=PSB[:, 4 + hh // 8, (hh % 8) * 128:(hh % 8) * 128 + 128]: h.copy(d, s))
        ptevdone[b] = pev
        self.w("tensor", "ss", pev)
        if b >= 1:
            self.w("tensor", "sv", otdone[b - 1])
        ov = None
        horder = list(range(0, 16, 2)) + list(range(1, 16, 2))
        for idx, hh in enumerate(horder):
            ht = hh // 2
            fn = lambda h, p=self.PS[64 * (hh % 2):64 * (hh % 2) + 64, 6 + ht // 4, (ht % 4) * 128:(ht % 4) * 128 + 128], \
                l=vhb[:, b, hh * 64:(hh + 1) * 64], r=pTs[:, hh, :]: \
                h.matmul(p, l, r, start=True, stop=True)
            if idx == 15:
                ov = self.op_inc("tensor", fn, "sp", 1)
            else:
                self.op("tensor", fn)
        self.w("vector", "sp", ov)
        done = None
        for ht in range(8):
            done = self.op("vector", lambda h, d=oT[:, ht, b * 128:(b + 1) * 128],
                           s=self.PS[:, 6 + ht // 4, (ht % 4) * 128:(ht % 4) * 128 + 128]:
                           h.tensor_copy(d, s))
        otdone[b] = done
    if stage.startswith("attn"):
        self.release()
        return
    ao = self.carve(BF16, (8, WO))
    self.mark()
    wbuf2 = self.carve(BF16, (2, 8, 128))
    self.barrier(("gpsimd", "tensor", "scalar"))
    evg2 = {}
    for og in range(8):
        slot = og % 2
        bank0 = (og % 2) * 2
        if og >= 2:
            self.w("gpsimd", "ss", evg2[og - 2])
        gv = self.op_inc("gpsimd", lambda h, d=wbuf2[:, slot], og=og: h.dma_start(
            out=d, in_=io["b3_wo"][og]), f"q{12 + slot}", 16)
        self.w("tensor", f"q{12 + slot}", gv)
        if og >= 2:
            self.w("tensor", "ss", evg2[og - 2])
        pv = None
        for ci, (o, wd) in enumerate(cho):
            for k in range(8):
                fn = lambda h, p=self.PS[:, bank0 + ci, 0:wd], l=wbuf2[:, slot, k, :], \
                    r=oT[:, k, o:o + wd], st=(k == 0), sp_=(k == 7): \
                    h.matmul(p, l, r, start=st, stop=sp_)
                if ci == 1 and k == 7:
                    pv = self.op_inc("tensor", fn, "sp", 1)
                else:
                    self.op("tensor", fn)
        self.w("scalar", "sp", pv)
        ev = None
        for ci, (o, wd) in enumerate(cho):
            ev = self.op("scalar", lambda h, d=ao[:, og, o:o + wd],
                         s=self.PS[:, bank0 + ci, 0:wd], og=og: h.activation(
                d, s, AF.Identity, bias=bof[:, og:og + 1]))
        evg2[og] = ev
    self.release()

    self.mark()
    wbuf3 = self.carve(BF16, (2, 8, 128))
    self.barrier(("gpsimd", "tensor", "scalar"))
    evg3 = {}
    stv3 = {}
    for og in range(32):
        slot = og % 2
        bank0 = (og % 2) * 2
        if og >= 2:
            self.w("gpsimd", "ss", evg3[og - 2])
        gv = self.op_inc("gpsimd", lambda h, d=wbuf3[:, slot], og=og: h.dma_start(
            out=d, in_=io["b3_pw"][og]), f"q{12 + slot}", 16)
        self.w("tensor", f"q{12 + slot}", gv)
        if og >= 2:
            self.w("tensor", "ss", evg3[og - 2])
        pv = None
        for ci, (o, wd) in enumerate(cho):
            for k in range(8):
                fn = lambda h, p=self.PS[:, bank0 + ci, 0:wd], l=wbuf3[:, slot, k, :], \
                    r=ao[:, k, o:o + wd], st=(k == 0), sp_=(k == 7): \
                    h.matmul(p, l, r, start=st, stop=sp_)
                if ci == 1 and k == 7:
                    pv = self.op_inc("tensor", fn, "sp", 1)
                else:
                    self.op("tensor", fn)
        self.w("scalar", "sp", pv)
        if og >= 2:
            self.w("scalar", f"q{8 + slot}", stv3[og - 2])
        ev = None
        for ci, (o, wd) in enumerate(cho):
            ev = self.op("scalar", lambda h, d=self.ring_a[:, slot, o:o + wd],
                         s=self.PS[:, bank0 + ci, 0:wd], og=og: h.activation(
                d, s, AF.Identity, bias=pbf[:, og:og + 1]))
        evg3[og] = ev
        self.w("scalar", "ss", ev)
        stv3[og] = self.op_inc("scalar", lambda h, d=out_d[og][:, 0:WO],
                               s=self.ring_a[:, slot, 0:WO]: h.dma_start(out=d, in_=s),
                               f"q{8 + slot}", 16)
    self.release()
    self.release()

    otile = lambda kt: out_d[kt][:, 0:WO]
    xtile_o = lambda kt: xfm[kt * 128:(kt + 1) * 128, HALO:HALO + WO]
    self.emit_fm_stats(otile, KD, WO, False)
    self.finalize_stats(WO)
    self.emit_resid(xtile_o, otile, KD, WO,
                    lambda kt: xout[kt * 128:(kt + 1) * 128, :], next_stats=True)


def prep_b3(inp):
    cw = inp["L1_attn_conv_w"]
    cwt = np.empty((24, 3, 128, KD * 128), np.float32)
    for og in range(24):
        for j in range(3):
            blkw = cw[j][:, og * 128:(og + 1) * 128]
            cwt[og, j] = blkw.reshape(KD, 128, 128).transpose(1, 0, 2).reshape(128, KD * 128)
    wq = np.ascontiguousarray(inp["L1_attn_wq"].reshape(1024, 1024))
    wk = np.ascontiguousarray(inp["L1_attn_wk"].reshape(1024, 1024))
    wv = np.ascontiguousarray(inp["L1_attn_wv"].reshape(1024, 1024))
    wo = np.ascontiguousarray(inp["L1_attn_wo"].reshape(1024, 1024))
    pw = np.ascontiguousarray(inp["L1_attn_proj_w"])
    wvt = wv.reshape(8, 128, 1024).transpose(1, 0, 2).reshape(128, 8 * 1024)
    return {
        "b3_cw": np.ascontiguousarray(cwt),
        "b3_cb": np.ascontiguousarray(inp["L1_attn_conv_b"].reshape(24, 128).T),
        "b3_wq": tile_lhsT(wq, 256),
        "b3_wk": tile_lhsT(wk, 256),
        "b3_wv": np.ascontiguousarray(wvt),
        "b3_wo": tile_lhsT(wo, 128),
        "b3_pw": tile_lhsT(pw, 128),
        "b3_bq": np.ascontiguousarray(inp["L1_attn_bq"].reshape(-1).reshape(8, 128).T),
        "b3_bk": np.ascontiguousarray(inp["L1_attn_bk"].reshape(-1).reshape(8, 128).T),
        "b3_bv": np.ascontiguousarray(inp["L1_attn_bv"].reshape(1, 1024)),
        "b3_bo": np.ascontiguousarray(inp["L1_attn_bo"].reshape(8, 128).T),
        "b3_pb": np.ascontiguousarray(inp["L1_attn_proj_b"].reshape(32, 128).T),
    }


def build_b3(width=TW, stage="full"):
    nc = bass.Bass("TRN2", target_bir_lowering=False)
    io = {}
    io["xin"] = nc.dram_tensor("xin", [4096, width], F32, kind="ExternalInput").ap()
    io["xout"] = nc.dram_tensor("xout", [4096, OWN], F32, kind="ExternalOutput").ap()
    io["b_out"] = nc.dram_tensor("b_out", [32, 128, width], BF16, kind="Internal").ap()
    io["b3_cw"] = nc.dram_tensor("b3_cw", [24, 3, 128, KD * 128], F32, kind="ExternalInput").ap()
    io["b3_cb"] = nc.dram_tensor("b3_cb", [128, 24], F32, kind="ExternalInput").ap()
    io["b3_wq"] = nc.dram_tensor("b3_wq", [4, 128, 8 * 256], F32, kind="ExternalInput").ap()
    io["b3_wk"] = nc.dram_tensor("b3_wk", [4, 128, 8 * 256], F32, kind="ExternalInput").ap()
    io["b3_wv"] = nc.dram_tensor("b3_wv", [128, 8 * 1024], F32, kind="ExternalInput").ap()
    io["b3_wo"] = nc.dram_tensor("b3_wo", [8, 128, 8 * 128], F32, kind="ExternalInput").ap()
    io["b3_pw"] = nc.dram_tensor("b3_pw", [32, 128, 8 * 128], F32, kind="ExternalInput").ap()
    for nm, shp in (("b3_bq", [128, 8]), ("b3_bk", [128, 8]), ("b3_bo", [128, 8]),
                    ("b3_pb", [128, 32]), ("b3_bv", [1, 1024]), ("b3_edge", [128, 2])):
        io[nm] = nc.dram_tensor(nm, shp, F32, kind="ExternalInput").ap()
    b = B(nc, width)
    b.emit_init()
    b.emit_block3(io, stage=stage)
    b.barrier(("sync",))
    with nc.Block() as block:
        b.build(block)
    return nc


B.emit_block3 = emit_block3


def emit_block4(self, io, stage="full", pre_stats_done=False):
    WO = OWN
    cho = chunks(WO)
    xfm = io["xin"]
    xout = io["xout"]
    out_d = io["b_out"]
    xln_d = io["b4_xln"]

    self.mark()
    self.barrier(("gpsimd",))
    xe = self.carve_high(BF16, (16, WO))
    xo = self.carve_high(BF16, (16, WO))
    sc_fm = self.carve_high(F32, (32,))
    ZB = self.carve_high(BF16, (4096,))
    CL = self.carve_high(BF16, (WO,))
    apb = self.carve_high(BF16, (KD, 16))
    mz = self.op("vector", lambda h, d=ZB: h.memset(d, 0.0))
    self.w("gpsimd", "sv", mz)
    self.op_inc("gpsimd", lambda h, d=sc_fm, s=io["b4_sc"]: h.dma_start(out=d, in_=s), "q14", 16)
    self.op_inc("gpsimd", lambda h, d=ZB[0:1], s=io["b4_z"]: h.dma_start(out=d, in_=s), "q14", 16)
    self.op_inc("gpsimd", lambda h, d=ZB[32:48], s=io["b4_bt"]: h.dma_start(out=d, in_=s), "q14", 16)
    self.op_inc("gpsimd", lambda h, d=apb.rearrange("p a b -> p (a b)"), s=io["b4_at"]:
                h.dma_start(out=d, in_=s), "q15", 16)

    def xtile(kt):
        return xfm[kt * 128:(kt + 1) * 128, :]

    xsc = self.carve_high(BF16, (4, WO))
    if not pre_stats_done:
        self.emit_fm_stats(xtile, KD, WO, True)
    self.finalize_stats(WO)
    self.emit_ln_apply(xtile, KD, WO, lambda kt: xsc[:, kt % 4, :],
                       also_dram_fn=lambda kt: xln_d[kt * 128:(kt + 1) * 128, :])
    self.barrier(("sync", "tensor", "vector", "scalar"))
    for jt in range(16):
        self.op_inc("sync", lambda h, d=xe[:, jt, :], jt=jt: h.dma_start(
            out=d, in_=xln_d[jt * 256:(jt + 1) * 256:2, :]), f"q{jt % 2}", 16)
        self.op_inc("sync", lambda h, d=xo[:, jt, :], jt=jt: h.dma_start(
            out=d, in_=xln_d[jt * 256 + 1:(jt + 1) * 256:2, :]), f"q{2 + jt % 2}", 16)
    self.barrier(("tensor", "vector", "scalar"))

    def eo(k):
        return xe[:, k, :] if k < 16 else xo[:, k - 16, :]

    pv = None
    for ci, (o, wd) in enumerate(cho):
        for k in range(32):
            self.op("tensor", lambda h, p=self.PS[0:1, ci, 0:wd], r=eo(k)[:, o:o + wd],
                    st=(k == 0), sp_=(k == 31): h.matmul(p, self.ones_col, r, start=st, stop=sp_))
    for ci, (o, wd) in enumerate(cho):
        for k in range(32):
            fn = lambda h, p=self.PS[0:16, 2 + ci, 0:wd], l=apb[:, k, :], r=eo(k)[:, o:o + wd], \
                st=(k == 0), sp_=(k == 31): h.matmul(p, l, r, start=st, stop=sp_)
            if ci == 1 and k == 31:
                pv = self.op_inc("tensor", fn, "sp", 1)
            else:
                self.op("tensor", fn)
    self.w("scalar", "sp", pv)
    for ci, (o, wd) in enumerate(cho):
        self.op("scalar", lambda h, d=CL[0:1, o:o + wd], s=self.PS[0:1, ci, 0:wd]: h.copy(d, s))
    ev = None
    for ci, (o, wd) in enumerate(cho):
        ev = self.op("scalar", lambda h, d=CL[32:48, o:o + wd], s=self.PS[0:16, 2 + ci, 0:wd]:
                     h.activation(d, s, AF.Identity, scale=2.0))
    self.w("tensor", "ss", ev)

    qbuf = self.carve(U8, (2, 16, 128))
    y2b = self.carve(F32, (2, 2048))
    tb = self.carve(F32, (2, 2048))
    hbw = self.carve(BF16, (4, 16, 128))
    lbw = self.carve(BF16, (4, 16, 128))
    self.barrier(("sync", "gpsimd", "tensor", "vector", "scalar"))
    M23 = 12582912.0
    qW = io["b4_qw"]
    vdone = {}
    sdone = {}
    stv = {}
    pdone = {}
    edone = {}
    for pr in range(16):
        ogs = [2 * pr, 2 * pr + 1]
        for og in ogs:
            qs = og % 2
            if pr >= 1:
                self.w("sync", "ss", sdone[pr - 1])
            self.op_inc("sync", lambda h, d=qbuf[:, qs].rearrange("p a b -> p (a b)"),
                        og=og: h.dma_start(out=d, in_=qW[og]), f"q{4 + qs}", 16)
        for og in ogs:
            qs = og % 2
            self.w("scalar", f"q{4 + qs}", self.sems[f"q{4 + qs}"].n)
            if pr >= 1:
                self.w("scalar", "sv", vdone[pr - 1])
            sd = self.op("scalar", lambda h, d=y2b[:, qs, :],
                         s=qbuf[:, qs].rearrange("p a b -> p (a b)"):
                         h.activation(d, s, AF.Identity, scale=1.0 / 16.0))
        sdone[pr] = sd
        for og in ogs:
            qs = og % 2
            self.w("vector", "ss", sdone[pr] - len(ogs) + (og - ogs[0]) + 1)
            if pr >= 2:
                self.w("vector", "sp", pdone[pr - 2])
            self.op("vector", lambda h, d=tb[:, qs, :], s=y2b[:, qs, :]:
                    h.tensor_scalar(d, s, -0.46875, M23, op0=ALU.add, op1=ALU.add))
        for og in ogs:
            qs = og % 2
            slot = og % 4
            self.w("vector", "sv", self.sems["sv"].n - len(ogs) + (og - ogs[0]) + 1)
            self.op("vector", lambda h, d=hbw[:, slot].rearrange("p a b -> p (a b)"),
                    s=tb[:, qs, :]: h.tensor_scalar(d, s, -M23, None, op0=ALU.add))
        for og in ogs:
            qs = og % 2
            slot = og % 4
            self.w("vector", "sv", self.sems["sv"].n - len(ogs) + (og - ogs[0]) + 1)
            self.op("vector", lambda h, d=tb[:, qs, :], s0=y2b[:, qs, :],
                    s1=hbw[:, slot].rearrange("p a b -> p (a b)"): h.scalar_tensor_tensor(
                d, s0, 0.0, s1, op0=ALU.add, op1=ALU.subtract))
        vd = None
        for og in ogs:
            qs = og % 2
            slot = og % 4
            self.w("vector", "sv", self.sems["sv"].n - len(ogs) + (og - ogs[0]) + 1)
            vd = self.op("vector", lambda h, d=lbw[:, slot].rearrange("p a b -> p (a b)"),
                         s=tb[:, qs, :]: h.tensor_scalar(d, s, 16.0, None, op0=ALU.mult))
        vdone[pr] = vd
        self.w("tensor", "sv", vdone[pr])
        if pr >= 1:
            self.w("tensor", "sv", edone[pr - 1])
        pv2s = []
        for og in ogs:
            slot = og % 4
            bank0 = (og % 2) * 4
            pv2 = None
            for ci, (o, wd) in enumerate(cho):
                for k in range(32):
                    lh = (lambda h, p=self.PS[:, bank0 + ci, 0:wd],
                          l=(hbw if k < 16 else lbw)[:, slot, k % 16, :],
                          r=eo(k)[:, o:o + wd], st=(k == 0), sp_=(k == 31):
                          h.matmul(p, l, r, start=st, stop=sp_))
                    self.op("tensor", lh)
            for ci, (o, wd) in enumerate(cho):
                fn = lambda h, p=self.PS[:, bank0 + 2 + ci, 0:wd], l=ZB[0:48, og * 128:(og + 1) * 128], \
                    r=CL[0:48, o:o + wd]: h.matmul(p, l, r, start=True, stop=True)
                if ci == 1:
                    pv2 = self.op_inc("tensor", fn, "sp", 1)
                else:
                    self.op("tensor", fn)
            pv2s.append(pv2)
        pdone[pr] = pv2s[-1]
        ed = None
        for gi, og in enumerate(ogs):
            es = og % 2
            bank0 = (og % 2) * 4
            self.w("vector", "sp", pv2s[gi])
            if pr >= 1:
                self.w("vector", f"q{8 + es}", stv[(pr - 1, es)])
            for ci, (o, wd) in enumerate(cho):
                self.op("vector", lambda h, d=self.ring_c[:, es, o:o + wd],
                        s=self.PS[:, bank0 + ci, 0:wd], og=og: h.tensor_scalar(
                    d, s, sc_fm[:, og:og + 1], None, op0=ALU.mult))
                ed = self.chain("vector", lambda h, d=self.ring_a[:, es, o:o + wd],
                                s0=self.ring_c[:, es, o:o + wd], s1=self.PS[:, bank0 + 2 + ci, 0:wd]:
                                h.tensor_tensor(d, s0, s1, op=ALU.add))
            self.w("scalar", "sv", ed)
            stv[(pr, es)] = self.op_inc("scalar", lambda h, d=out_d[og][:, 0:WO],
                                        s=self.ring_a[:, es, 0:WO]: h.dma_start(out=d, in_=s),
                                        f"q{8 + es}", 16)
        edone[pr] = ed
    self.release_high()
    self.release()

    otile = lambda kt: out_d[kt][:, 0:WO]
    self.emit_fm_stats(otile, KD, WO, False)
    self.finalize_stats(WO)
    self.emit_resid(xtile, otile, KD, WO,
                    lambda kt: xout[kt * 128:(kt + 1) * 128, :])


def prep_b4(inp):
    qw = inp["L1_ffn_qweight"]
    qb = np.ascontiguousarray(qw.view(np.uint8).reshape(4096, 2048, 4)[:, :, 0])
    qbT = np.ascontiguousarray(qb.T)
    qt = np.empty((32, 128, 16 * 128), np.uint8)
    for og in range(32):
        sl = qbT[:, og * 128:(og + 1) * 128]
        qt[og] = sl.reshape(16, 128, 128).transpose(1, 0, 2).reshape(128, 16 * 128)
    At = np.ascontiguousarray(inp["L1_ffn_lora_A"].T)
    Ap = np.concatenate([At[0::2], At[1::2]], axis=0)
    apt = Ap.reshape(KD, 128, 16).transpose(1, 0, 2).reshape(128, KD * 16)
    return {
        "b4_qw": qt,
        "b4_sc": np.ascontiguousarray(inp["L1_ffn_scales"].astype(np.float32).reshape(32, 128).T),
        "b4_z": np.ascontiguousarray(inp["L1_ffn_zeros"].reshape(1, 4096)),
        "b4_bt": np.ascontiguousarray(inp["L1_ffn_lora_B"].T),
        "b4_at": np.ascontiguousarray(apt),
    }


def build_b4(stage="full"):
    nc = bass.Bass("TRN2", target_bir_lowering=False)
    io = {}
    io["xin"] = nc.dram_tensor("xin", [4096, OWN], F32, kind="ExternalInput").ap()
    io["xout"] = nc.dram_tensor("xout", [4096, OWN], F32, kind="ExternalOutput").ap()
    io["b_out"] = nc.dram_tensor("b_out", [32, 128, OWN], BF16, kind="Internal").ap()
    io["b4_xln"] = nc.dram_tensor("b4_xln", [4096, OWN], BF16, kind="Internal").ap()
    io["b4_qw"] = nc.dram_tensor("b4_qw", [32, 128, 16 * 128], U8, kind="ExternalInput").ap()
    io["b4_sc"] = nc.dram_tensor("b4_sc", [128, 32], F32, kind="ExternalInput").ap()
    io["b4_z"] = nc.dram_tensor("b4_z", [1, 4096], F16, kind="ExternalInput").ap()
    io["b4_bt"] = nc.dram_tensor("b4_bt", [16, 4096], F32, kind="ExternalInput").ap()
    io["b4_at"] = nc.dram_tensor("b4_at", [128, KD * 16], F32, kind="ExternalInput").ap()
    b = B(nc, OWN)
    b.emit_init()
    b.emit_block4(io, stage=stage)
    b.barrier(("sync",))
    with nc.Block() as block:
        b.build(block)
    return nc


B.emit_block4 = emit_block4


from concourse.bass_utils import run_bass_kernel_spmd


def build_fused():
    nc = bass.Bass("TRN2", target_bir_lowering=False)
    io = {}
    io["xin"] = nc.dram_tensor("xin", [4096, TW], F32, kind="ExternalInput").ap()
    io["xout"] = nc.dram_tensor("xout", [4096, OWN], F32, kind="ExternalOutput").ap()
    xq1 = nc.dram_tensor("xq1", [4096, TW], F32, kind="Internal").ap()
    xq2 = nc.dram_tensor("xq2", [4096, TW], F32, kind="Internal").ap()
    xq3 = nc.dram_tensor("xq3", [4096, OWN], F32, kind="Internal").ap()
    io["b_out"] = nc.dram_tensor("b_out", [32, 128, TW], BF16, kind="Internal").ap()
    io["b1_out"] = io["b_out"]
    io["b1_h"] = nc.dram_tensor("b1_h", [78, 128, TW], BF16, kind="Internal").ap()
    io["b1_ud"] = nc.dram_tensor("b1_ud", [2, 128, 8 * TW], BF16, kind="Internal").ap()
    io["b4_xln"] = nc.dram_tensor("b4_xln", [4096, OWN], BF16, kind="Internal").ap()
    for nm, shape, dt_ in (("b1_up", [8, 128, KD * 128], F32), ("b1_gate", [8, 128, KD * 128], F32),
                           ("b1_gw", [78, 128, KD * 128], F32), ("b1_uw", [78, 128, KD * 128], F32),
                           ("b1_dw", [16, 128, 78 * 256], F32), ("b1_apxA", [32, 128, 24 * 128], F32),
                           ("b1_apxB", [32, 128, 32 * 128], F32), ("b1_lp", [128, 8], F32),
                           ("b2_rw", [128, KD * 64], F32), ("b2_rb", [64, 1], F32),
                           ("b2_eb", [64, 4096], F32),
                           ("b3_cw", [24, 3, 128, KD * 128], F32), ("b3_cb", [128, 24], F32),
                           ("b3_wq", [4, 128, 8 * 256], F32), ("b3_wk", [4, 128, 8 * 256], F32),
                           ("b3_wv", [128, 8 * 1024], F32), ("b3_wo", [8, 128, 8 * 128], F32),
                           ("b3_pw", [32, 128, 8 * 128], F32),
                           ("b3_bq", [128, 8], F32), ("b3_bk", [128, 8], F32),
                           ("b3_bo", [128, 8], F32), ("b3_pb", [128, 32], F32),
                           ("b3_bv", [1, 1024], F32), ("b3_edge", [128, 2], F32),
                           ("b4_qw", [32, 128, 16 * 128], U8), ("b4_sc", [128, 32], F32),
                           ("b4_z", [1, 4096], F16), ("b4_bt", [16, 4096], F32),
                           ("b4_at", [128, KD * 16], F32)):
        io[nm] = nc.dram_tensor(nm, shape, dt_, kind="ExternalInput").ap()

    b = B(nc, TW)
    b.emit_init()
    io1 = dict(io); io1["xin"] = io["xin"]; io1["xout"] = xq1
    b.emit_block1(io1)
    io2 = dict(io); io2["xin"] = xq1; io2["xout"] = xq2
    b.emit_block2(io2, pre_stats_done=True)
    io3 = dict(io); io3["xin"] = xq2; io3["xout"] = xq3
    b.emit_block3(io3, pre_stats_done=True)
    io4 = dict(io); io4["xin"] = xq3; io4["xout"] = io["xout"]
    b.emit_block4(io4, pre_stats_done=True)
    b.barrier(("sync",))
    with nc.Block() as block:
        b.build(block)
    return nc


def kernel(**inputs):
    inputs = {k: np.asarray(v) for k, v in inputs.items()}
    w = {}
    w.update(prep_b1(inputs))
    w.update(prep_b2(inputs))
    w.update(prep_b3(inputs))
    w.update(prep_b4(inputs))
    x = np.ascontiguousarray(inputs["L0_attn_hidden_states"].astype(np.float32, copy=False))
    xs = core_x_slices(x)
    in_maps = []
    for c in range(8):
        e = np.zeros((128, 2), np.float32)
        e[:, 0] = 1.0 if c % 2 == 1 else 0.0
        e[:, 1] = 1.0 if c % 2 == 0 else 0.0
        m = dict(w)
        m["xin"] = xs[c]
        m["b3_edge"] = e
        in_maps.append(m)
    nc = build_fused()
    res = run_bass_kernel_spmd(nc, in_maps, core_ids=list(range(8)))
    cur = [res.results[c]["xout"] for c in range(8)]
    full = np.concatenate(cur, axis=1)
    return np.ascontiguousarray(full.T).reshape(4, 2048, 4096).astype(np.float32, copy=False)
